# version 23 (final)
# baseline (speedup 1.0000x reference)
import math
import numpy as np
from contextlib import ExitStack

import concourse.bass as bass
import concourse.mybir as mybir
from concourse.bass_utils import run_bass_kernel_spmd

F32 = mybir.dt.float32
BF16 = mybir.dt.bfloat16
I32 = mybir.dt.int32
AF = mybir.ActivationFunctionType
ALU = mybir.AluOpType
AX = mybir.AxisListType

DEBUG_FPROC = False
CUT = None
CUT_HEADS = None
DBG_SCRATCH = False
FULL_CFG = dict(S=2048, D=2048, QR=1024, KVR=512, NH=16, DFF=8192, KSEL=512)
NOPE, ROPE, VD = 128, 64, 128
DH = NOPE + ROPE
LN_EPS = 1e-5
RMS_EPS = 1e-6
ROPE_BASE = 10000.0
TWO_PI = 2.0 * math.pi


def host_consts(C):
    KSEL = C["KSEL"]
    SC = KSEL // 128
    ident32 = np.eye(128, dtype=np.float32)
    dmasks = np.zeros((4, 128, 512), dtype=np.float32)
    kk = np.arange(128)[:, None]
    qq = np.arange(512)[None, :]
    for d in range(4):
        dmasks[d] = (qq >= 128 * d + kk).astype(np.float32)
    iotak = np.broadcast_to(np.arange(KSEL, dtype=np.float32), (128, KSEL)).copy()
    iotacols = (np.arange(128, dtype=np.float32)[:, None]
                + 128.0 * np.arange(SC, dtype=np.float32)[None, :]).copy()
    S = C["S"]
    NH = C["NH"]
    half = 32
    inv = (1.0 / 10000.0 ** (np.arange(half, dtype=np.float32) * 2.0
                             / 64.0)).astype(np.float32)
    pos = np.arange(S, dtype=np.float32)
    ang = (inv[:, None] * pos[None, :]).astype(np.float32)
    cosq = np.cos(ang).astype(np.float32)
    sinq = np.sin(ang).astype(np.float32)
    tabcs = np.empty((128, 4, S), dtype=np.float32)
    tabcs[:, 0, :] = np.tile(cosq, (4, 1))
    tabcs[:, 1, :] = np.tile(sinq, (4, 1))
    tabcs[:, 2, :] = np.tile(cosq / NH, (4, 1))
    tabcs[:, 3, :] = np.tile(sinq / NH, (4, 1))
    return dict(ident32=ident32, dmasks=dmasks, iotak=iotak, iotacols=iotacols,
                tabcs=tabcs)



class VG:
    _OPS = {
        "tensor_scalar": (0, (1, 2), ("accum_out",)),
        "tensor_single_scalar": (0, (1, 2), ()),
        "tensor_tensor": (0, (1, 2), ()),
        "tensor_add": (0, (1, 2), ()),
        "tensor_sub": (0, (1, 2), ()),
        "tensor_mul": (0, (1, 2), ()),
        "tensor_max": (0, (1, 2), ()),
        "tensor_copy": ("out", ("in_",), ()),
        "reduce_max": ("out", ("in_",), ()),
        "reduce_sum": ("out", ("in_",), ()),
        "tensor_reduce": (0, (1,), ()),
        "scalar_tensor_tensor": (0, (1, 2, 3), ("accum_out",)),
        "reciprocal": (0, (1,), ()),
        "tensor_tensor_scan": (0, (1, 2, 3), ()),
        "memset": (0, (), ()),
        "select": (0, (1, 2, 3), ()),
        "bn_stats": ("out", ("in_",), ()),
        "bn_aggr": ("out", ("in_",), ()),
    }
    WINDOW = 8

    def __init__(self, v):
        self._v = v
        self._hist = []

    def __getattr__(self, name):
        f = getattr(self._v, name)
        spec = self._OPS.get(name)
        if spec is None:
            return f
        wpos, rpos, wkw = spec

        def _name_of(x):
            try:
                return x.tensor.name
            except Exception:
                return None

        def wrapped(*args, **kwargs):
            def get(p):
                if isinstance(p, int):
                    return args[p] if p < len(args) else None
                return kwargs.get(p)
            writes = set()
            reads = set()
            w = get(wpos) if not isinstance(wpos, str) else kwargs.get(wpos, args[0] if args else None)
            n = _name_of(w)
            if n:
                writes.add(n)
            for p in rpos:
                n = _name_of(get(p))
                if n:
                    reads.add(n)
            for p in wkw:
                n = _name_of(kwargs.get(p))
                if n:
                    writes.add(n)
            conflict = False
            for hw, hr in self._hist:
                if (reads & hw) or (writes & hw) or (writes & hr):
                    conflict = True
                    break
            if conflict:
                self._v.drain()
                self._hist.clear()
            ins = f(*args, **kwargs)
            self._hist.append((writes, reads))
            if len(self._hist) > self.WINDOW:
                self._hist.pop(0)
            return ins
        return wrapped


class Ctr:
    def __init__(self):
        self.v = {}
    def add(self, k, n=1):
        self.v[k] = self.v.get(k, 0) + n
        return self.v[k]
    def get(self, k):
        return self.v.get(k, 0)



def _audit(nc, c, mapping):
    import collections
    got = collections.Counter()
    for ins in nc.all_instructions():
        si = getattr(ins, "sync_info", None)
        ups = getattr(si, "on_update", None) if si is not None else None
        if not ups:
            continue
        for u in ups:
            nm = str(getattr(u, "ant_name", ""))
            if nm in mapping:
                got[nm] += int(getattr(u, "update_value", 1) or 1)
    bad = []
    for key, ck in mapping.items():
        exp = c.get(ck)
        if got.get(key, 0) != exp:
            bad.append(f"{key}: emitted {got.get(key,0)} expected {exp}")
    if bad:
        raise AssertionError("sem audit: " + "; ".join(bad))


def _finish(nc, c, ctx, stacks, s_lds, s_ldg, s_out, ncd):
    with nc.Block() as blk:
        @blk.sync
        def _(sy):
            sy.wait_ge(s_lds, c.get("lds"))
            if c.get("out"):
                sy.wait_ge(s_out, c.get("out"))

        @blk.gpsimd
        def _(g):
            g.wait_ge(s_ldg, c.get("ldg"))
    for st_ in stacks:
        st_.close()
    ctx.close()
    ncd.__exit__(None, None, None)
    _audit(nc, c, {"s_lds": "lds", "s_ldg": "ldg", "s_mm": "mm", "s_ve": "ve",
                   "s_ac": "ac", "s_gp": "gp", "s_out": "out"})
    return nc


def build_nc(C):
    S, D, QR, KVR, NH, DFF, KSEL = (C["S"], C["D"], C["QR"], C["KVR"],
                                    C["NH"], C["DFF"], C["KSEL"])
    TT, DC, QC, KC, FC = S // 128, D // 128, QR // 128, KVR // 128, DFF // 128
    SP = S // 512
    SC = KSEL // 128
    HG = NH // 4
    DSP = D // 512

    nc = bass.Bass("TRN2", target_bir_lowering=False, debug=False,
                   detect_race_conditions=False)
    _ncd = nc.allow_non_contiguous_dma(reason="small strided const/stat loads")
    _ncd.__enter__()

    din = lambda n, shp: nc.dram_tensor(n, shp, F32, kind="ExternalInput").ap()
    x_in = din("x", [S, D])
    W_cq, q_scale = din("W_cq", [D, QR]), din("q_scale", [QR])
    W_dq_nope, W_dq_rope = din("W_dq_nope", [QR, NH * NOPE]), din("W_dq_rope", [QR, NH * ROPE])
    W_ckv, kv_scale = din("W_ckv", [D, KVR]), din("kv_scale", [KVR])
    W_dk_nope, W_dv = din("W_dk_nope", [KVR, NH * NOPE]), din("W_dv", [KVR, NH * VD])
    W_krope, W_o = din("W_krope", [D, ROPE]), din("W_o", [NH * VD, D])
    W_r, b_r = din("W_r", [D, 1]), din("b_r", [1])
    W1, b1 = din("W1", [D, DFF]), din("b1", [DFF])
    W2, b2 = din("W2", [DFF, D]), din("b2", [D])
    ident32 = din("ident32", [128, 128])
    dmasks = din("dmasks", [4, 128, 512])
    tabcs = din("tabcs", [128, 4, C["S"]])
    iotak = din("iotak", [128, KSEL])
    iotacols = din("iotacols", [128, SC])

    out = nc.dram_tensor("out", [S, D], F32, kind="ExternalOutput").ap()

    scr = lambda n, shp, dt: nc.dram_tensor(
        n, shp, dt, kind=("ExternalOutput" if DBG_SCRATCH else "Internal")).ap()
    x1_s = scr("x1_s", [S, D], BF16)
    x2_s = scr("x2_s", [S, D], F32)
    o_s = scr("o_s", [TT, NH, VD, 128], BF16)
    y_s = scr("y_s", [S, D], BF16)
    vals_s = scr("vals_s", [D, KSEL], BF16)
    rt_s = scr("rt_s", [1, S], F32)
    grow_s = scr("grow_s", [2, S], BF16)
    rl_s = scr("rl_s", [2, 512], F32)
    wrow_s = scr("wrow_s", [1, S], F32)
    selrow_s = scr("selrow_s", [1, S], F32)
    wselrow_s = scr("wselrow_s", [1, KSEL], F32)
    w1b_s = scr("w1b_s", [D, DFF], BF16)
    w2b_s = scr("w2b_s", [DFF, D], BF16)
    wob_s = scr("wob_s", [NH * VD, D], BF16)
    adbg = scr("adbg", [128, D], F32) if DBG_SCRATCH else None
    otdbg = scr("otdbg", [128, 16], F32) if DBG_SCRATCH else None
    fdbg = scr("fdbg", [S // 128, 128, D], F32) if DEBUG_FPROC else None

    ctx = ExitStack()
    sb = lambda n, shp, dt=F32: ctx.enter_context(nc.sbuf_tensor(n, shp, dt))
    sem = lambda n: ctx.enter_context(nc.semaphore(n))

    idbf = sb("idbf", [128, 128], BF16)
    id32 = sb("id32", [128, 128], F32)
    iotak_sb = sb("iotak_sb", [128, KSEL], F32)
    iotacols_sb = sb("iotacols_sb", [128, SC], F32)
    ones_col = sb("ones_col", [128, 1], BF16)
    m2r2 = sb("m2r2", [128, TT], F32)
    r2col = sb("r2col", [128, TT], F32)
    selc = sb("selc", [128, TT], F32)
    nselc = sb("nselc", [128, TT], F32)
    w16 = sb("w16", [128, TT], F32)
    small = sb("small", [128, 16], F32)
    rowA = sb("rowA", [1, S], F32)
    rowB = sb("rowB", [1, S], F32)

    s_ldg = sem("s_ldg")
    s_lds = sem("s_lds")
    s_mm = sem("s_mm")
    s_ve = sem("s_ve")
    s_ac = sem("s_ac")
    s_gp = sem("s_gp")
    s_out = sem("s_out")
    c = Ctr()

    def v_stats(v, x_ap, junk_ap, bn2_ap, tmp1_ap, width):
        v.tensor_scalar(junk_ap, x_ap, 1.0, None, ALU.mult, ALU.add,
                        accum_out=bn2_ap[:, 0:1])
        v.scalar_tensor_tensor(junk_ap, x_ap, 1.0, x_ap, ALU.mult, ALU.mult,
                               accum_out=bn2_ap[:, 1:2])
        v.tensor_scalar(bn2_ap[:, 0:1], bn2_ap[:, 0:1], 1.0 / width, None, ALU.mult)
        v.tensor_scalar(bn2_ap[:, 1:2], bn2_ap[:, 1:2], 1.0 / width, None, ALU.mult)
        v.tensor_mul(tmp1_ap, bn2_ap[:, 0:1], bn2_ap[:, 0:1])
        v.tensor_tensor(bn2_ap[:, 1:2], bn2_ap[:, 1:2], tmp1_ap, ALU.subtract)

    def v_rsqrt(v, x_ap, s_ap, r_ap, t_ap):
        v.reciprocal(r_ap, s_ap)
        v.tensor_mul(t_ap, r_ap, r_ap)
        v.tensor_mul(t_ap, t_ap, x_ap)
        v.tensor_scalar(t_ap, t_ap, -0.5, 1.5, ALU.mult, ALU.add)
        v.tensor_mul(r_ap, r_ap, t_ap)

    ctx_att = ExitStack()
    asb = lambda n, shp, dt=F32: ctx_att.enter_context(nc.sbuf_tensor(n, shp, dt))
    cq_bf = asb("cq_bf", [128, QC, S], BF16)
    ckv_bf = asb("ckv_bf", [128, KC, S], BF16)
    kx_rope = asb("kx_rope", [65, S], BF16)
    tabs = asb("tabs", [128, 4, S], BF16)
    dmask_sb = asb("dmask_sb", [128, 4, 512], BF16)
    qscol = asb("qscol", [128, QC], F32)
    kvscol = asb("kvscol", [128, KC], F32)

    with nc.Block() as blk:
        blds = c.get("lds")

        @blk.gpsimd
        def _(g):
            g.dma_start(out=dmask_sb[:], in_=dmasks.rearrange("f p m -> p f m")).then_inc(s_ldg, 16)
            g.dma_start(out=tabs[:], in_=tabcs).then_inc(s_ldg, 16)
        c.add("ldg", 32)

        @blk.sync
        def _(sy):
            for ap_dst, ap_src in [
                (id32[:], ident32[:]),
                (iotak_sb[:], iotak[:]),
                (iotacols_sb[:], iotacols[:]),
                (qscol[:], q_scale.rearrange("(c p) -> p c", p=128)),
                (kvscol[:], kv_scale.rearrange("(c p) -> p c", p=128)),
                (rowA[0:1, 0:D], W_r.rearrange("d u -> u d")),
                (rowB[0:1, 0:1], b_r.unsqueeze(0)),
            ]:
                sy.dma_start(out=ap_dst, in_=ap_src).then_inc(s_lds, 16)
        c.add("lds", 7 * 16)

        @blk.vector
        def _(v):
            v = VG(v)
            v.memset(ones_col[:], 1.0)
            v.memset(kx_rope[64:65, :], 1.0)
            v.wait_ge(s_lds, blds + 7 * 16)
            v.tensor_copy(out=idbf[:], in_=id32[:])
        c.add("ve", 0)

    ctx_pre = ExitStack()
    psb = lambda n, shp, dt=F32: ctx_pre.enter_context(nc.sbuf_tensor(n, shp, dt))
    w_cq_bf = psb("w_cq_bf", [128, DC, QR], BF16)
    w_ckv_bf = psb("w_ckv_bf", [128, DC, KVR], BF16)
    w_kro_bf = psb("w_kro_bf", [128, DC, ROPE], BF16)
    x1f = psb("x1f", [128, 2, DC, 512], BF16)
    kev_raw = psb("kev_raw", [32, S], BF16)
    kod_raw = psb("kod_raw", [32, S], BF16)

    if CUT == "b0a":
        return _finish(nc, c, ctx, [ctx_att], s_lds, s_ldg, s_out, _ncd)

    with nc.Block() as blk, ExitStack() as lc:
        lsb = lambda n, shp, dt=F32: lc.enter_context(nc.sbuf_tensor(n, shp, dt))
        x0t = lsb("x0t", [128, 2, D], F32)
        x1t = lsb("x1t", [128, 2, D], BF16)
        bn6 = lsb("bn6", [128, 6 * (D // 512)], F32)
        bn2 = lsb("bn2", [128, 2], F32)
        rcol = lsb("rcol", [128, 1], F32)
        mrneg = lsb("mrneg", [128, 1], F32)
        bve, bac, blds = c.get("ve"), c.get("ac"), c.get("lds")

        @blk.gpsimd
        def _(g):
            g.dma_start(out=w_cq_bf[:], in_=W_cq.rearrange("(c p) m -> p c m", p=128)).then_inc(s_ldg, 16)
            g.dma_start(out=w_ckv_bf[:], in_=W_ckv.rearrange("(c p) m -> p c m", p=128)).then_inc(s_ldg, 16)
            g.dma_start(out=w_kro_bf[:], in_=W_krope.rearrange("(c p) m -> p c m", p=128)).then_inc(s_ldg, 16)
        c.add("ldg", 3 * 16)

        @blk.sync
        def _(sy):
            for tt in range(TT):
                if tt >= 2:
                    sy.wait_ge(s_ac, bac + 2 * (tt - 2) + 2)
                sy.dma_start(out=x0t[:, tt % 2, :], in_=x_in[tt * 128:(tt + 1) * 128, :]).then_inc(s_lds, 16)
                sy.wait_ge(s_ac, bac + 2 * tt + 2)
                sy.dma_start(out=x1_s[tt * 128:(tt + 1) * 128, :], in_=x1t[:, tt % 2, :]).then_inc(s_lds, 16)
        c.add("lds", TT * 32)

        @blk.vector
        def _(v):
            v = VG(v)
            for tt in range(TT):
                v.wait_ge(s_lds, blds + tt * 32 + 16)
                xt = x0t[:, tt % 2, :]
                if tt >= 2:
                    v.wait_ge(s_lds, blds + 32 * (tt - 2) + 32)
                v_stats(v, xt, x1t[:, tt % 2, :], bn2[:], small[:, 3:4], D)
                v.tensor_scalar(small[:, 0:1], bn2[:, 1:2], float(LN_EPS), None, ALU.add).then_inc(s_ve, 1)
                v.wait_ge(s_ac, bac + 2 * tt + 1)
                v_rsqrt(v, small[:, 0:1], small[:, 1:2], rcol[:], small[:, 2:3])
                v.tensor_mul(mrneg[:], bn2[:, 0:1], rcol[:])
                v.tensor_scalar(mrneg[:], mrneg[:], -1.0, None, ALU.mult).then_inc(s_ve, 1)
        c.add("ve", 2 * TT)

        @blk.scalar
        def _(sc):
            for tt in range(TT):
                sc.wait_ge(s_ve, bve + 2 * tt + 1)
                sc.activation(small[:, 1:2], small[:, 0:1], AF.Sqrt).then_inc(s_ac, 1)
                sc.wait_ge(s_ve, bve + 2 * tt + 2)
                if tt >= 2:
                    sc.wait_ge(s_lds, blds + 32 * (tt - 2) + 32)
                sc.activation(x1t[:, tt % 2, :], x0t[:, tt % 2, :], AF.Identity,
                              bias=mrneg[:], scale=rcol[:]).then_inc(s_ac, 1)
        c.add("ac", 2 * TT)

    if CUT == "b0b":
        return _finish(nc, c, ctx, [ctx_pre, ctx_att], s_lds, s_ldg, s_out, _ncd)

    with nc.Block() as blk, ExitStack() as lc:
        p_rot = [lc.enter_context(nc.psum_tensor("p_rot%d" % k, [128, 512], F32))
                 for k in range(4)]
        bmm, bac, bldg, blds = c.get("mm"), c.get("ac"), c.get("ldg"), c.get("lds")
        sjobs = ([("cq", qrc) for qrc in range(QC)]
                 + [("ckv", kc) for kc in range(KC)]
                 + [("kev", 0), ("kod", 0)])
        NJ = len(sjobs)

        @blk.sync
        def _(sy):
            sy.wait_ge(s_lds, blds)
            for sp in range(SP):
                if sp >= 2:
                    sy.wait_ge(s_mm, bmm + (sp - 1) * NJ)
                for dc in range(DC):
                    sy.dma_start_transpose(
                        out=x1f[:, sp % 2, dc, :],
                        in_=x1_s[sp * 512:(sp + 1) * 512,
                                 dc * 128:(dc + 1) * 128]).then_inc(s_lds, 16)
        c.add("lds", SP * DC * 16)

        @blk.tensor
        def _(t):
            t.wait_ge(s_ldg, bldg)
            for sp in range(SP):
                t.wait_ge(s_lds, blds + (sp + 1) * DC * 16)
                for j, (kind, cc) in enumerate(sjobs):
                    i = sp * NJ + j
                    if i >= 4:
                        t.wait_ge(s_ac, bac + i - 3)
                    for dc in range(DC):
                        rhs = x1f[:, sp % 2, dc, :]
                        if kind == "cq":
                            last = t.matmul(p_rot[i % 4][:], w_cq_bf[:, dc, cc * 128:(cc + 1) * 128],
                                            rhs, start=(dc == 0), stop=(dc == DC - 1))
                        elif kind == "ckv":
                            last = t.matmul(p_rot[i % 4][:], w_ckv_bf[:, dc, cc * 128:(cc + 1) * 128],
                                            rhs, start=(dc == 0), stop=(dc == DC - 1))
                        else:
                            wv = w_kro_bf[:, dc, :].rearrange("p (j two) -> p j two", two=2)
                            lhsT = wv[:, :, 0] if kind == "kev" else wv[:, :, 1]
                            last = t.matmul(p_rot[i % 4][0:32, :], lhsT, rhs,
                                            start=(dc == 0), stop=(dc == DC - 1))
                    last.then_inc(s_mm, 1)
        c.add("mm", SP * NJ)

        @blk.scalar
        def _(sc):
            for sp in range(SP):
                sl = slice(sp * 512, (sp + 1) * 512)
                for j, (kind, cc) in enumerate(sjobs):
                    i = sp * NJ + j
                    sc.wait_ge(s_mm, bmm + i + 1)
                    if kind == "cq":
                        dst, srcp = cq_bf[:, cc, sl], p_rot[i % 4][:]
                    elif kind == "ckv":
                        dst, srcp = ckv_bf[:, cc, sl], p_rot[i % 4][:]
                    elif kind == "kev":
                        dst, srcp = kev_raw[:, sl], p_rot[i % 4][0:32, :]
                    else:
                        dst, srcp = kod_raw[:, sl], p_rot[i % 4][0:32, :]
                    sc.activation(dst, srcp, AF.Identity).then_inc(s_ac, 1)
        c.add("ac", SP * NJ)

    if CUT == "b1":
        return _finish(nc, c, ctx, [ctx_pre, ctx_att], s_lds, s_ldg, s_out, _ncd)

    with nc.Block() as blk, ExitStack() as lc:
        lsb = lambda n, shp, dt=F32: lc.enter_context(nc.sbuf_tensor(n, shp, dt))
        p_g = [lc.enter_context(nc.psum_tensor("p_g%d" % k, [128, 128], F32))
               for k in range(2)]
        p_row = lc.enter_context(nc.psum_tensor("p_row", [1, S], F32))
        ssq = lsb("ssq", [128, TT], F32)
        sskv = lsb("sskv", [128, TT], F32)
        junk = lsb("junk", [128, 128], F32)
        gt = lsb("gt", [128, TT], F32)
        gs = lsb("gs", [128, TT], F32)
        gr = lsb("gr", [128, TT], F32)
        gsc = lsb("gsc", [128, TT], F32)
        g16b = lsb("g16b", [128, 2, TT], BF16)
        grow = lsb("grow", [1, 2, S], BF16)
        gbc = lsb("gbc", [128, 2, S], BF16)
        rt1 = lsb("rt1", [32, S], BF16)
        rt2 = lsb("rt2", [32, S], BF16)
        bmm, bve, bac, bgp = c.get("mm"), c.get("ve"), c.get("ac"), c.get("gp")
        blds_c, bldg_c = c.get("lds"), c.get("ldg")

        @blk.tensor
        def _(t):
            for i in range(2 * TT):
                kind, tt = ("q", i) if i < TT else ("kv", i - TT)
                if i >= 2:
                    t.wait_ge(s_ve, bve + i - 1)
                sl = slice(tt * 128, (tt + 1) * 128)
                nch, src = (QC, cq_bf) if kind == "q" else (KC, ckv_bf)
                for cc in range(nch):
                    last = t.matmul(p_g[i % 2][:], src[:, cc, sl], src[:, cc, sl],
                                    start=(cc == 0), stop=(cc == nch - 1))
                last.then_inc(s_mm, 1)
            for z in range(2):
                t.wait_ge(s_ve, bve + 2 * TT + (2 if z == 0 else 5))
                for tt in range(TT):
                    last = t.matmul(p_row[0:1, tt * 128:(tt + 1) * 128], g16b[:, z, tt:tt + 1],
                                    idbf[:], start=True, stop=True)
                last.then_inc(s_mm, 1)
        c.add("mm", 2 * TT + 2)

        @blk.scalar
        def _(sc):
            for z in range(2):
                sc.wait_ge(s_ve, bve + 2 * TT + (1 if z == 0 else 3))
                sc.activation(gs[:], gt[:], AF.Sqrt).then_inc(s_ac, 1)
        c.add("ac", 2)

        @blk.vector
        def _(v):
            v = VG(v)
            for i in range(2 * TT):
                kind, tt = ("q", i) if i < TT else ("kv", i - TT)
                v.wait_ge(s_mm, bmm + i + 1)
                dst = ssq if kind == "q" else sskv
                v.scalar_tensor_tensor(junk[:], p_g[i % 2][:], 1.0, id32[:],
                                       ALU.mult, ALU.mult, accum_out=dst[:, tt:tt + 1]).then_inc(s_ve, 1)
            for z, (ss, aa, bb) in enumerate([(ssq, float(DH) / QR, float(DH) * RMS_EPS),
                                              (sskv, 1.0 / KVR, RMS_EPS)]):
                v.tensor_scalar(gt[:], ss[:], aa, bb, ALU.mult, ALU.add).then_inc(s_ve, 1)
                v.wait_ge(s_ac, bac + z + 1)
                v_rsqrt(v, gt[:], gs[:], gr[:], gsc[:])
                v.tensor_copy(out=g16b[:, z, :], in_=gr[:]).then_inc(s_ve, 1)
            v.wait_ge(s_mm, bmm + 2 * TT + 1)
            v.tensor_copy(out=grow[0:1, 0, :], in_=p_row[0:1, :]).then_inc(s_ve, 1)
            v.wait_ge(s_mm, bmm + 2 * TT + 2)
            v.tensor_copy(out=grow[0:1, 1, :], in_=p_row[0:1, :]).then_inc(s_ve, 1)
            v.wait_ge(s_ldg, bldg_c + 32)
            for qrc in range(QC):
                v.tensor_tensor(cq_bf[:, qrc, :], cq_bf[:, qrc, :], gbc[:, 0, :], ALU.mult)
            for kc in range(KC):
                v.tensor_tensor(ckv_bf[:, kc, :], ckv_bf[:, kc, :], gbc[:, 1, :], ALU.mult)
            v.tensor_tensor(rt1[:], kev_raw[:], tabs[0:32, 2, :], ALU.mult)
            v.tensor_tensor(rt2[:], kod_raw[:], tabs[0:32, 3, :], ALU.mult)
            v.tensor_tensor(kx_rope[0:32, :], rt1[:], rt2[:], ALU.subtract)
            v.tensor_tensor(rt1[:], kev_raw[:], tabs[0:32, 3, :], ALU.mult)
            v.tensor_tensor(rt2[:], kod_raw[:], tabs[0:32, 2, :], ALU.mult)
            v.tensor_tensor(rt1[:], rt1[:], rt2[:], ALU.add)
            v.tensor_copy(out=kx_rope[32:64, :], in_=rt1[:])
        c.add("ve", 2 * TT + 6)

        @blk.sync
        def _(sy):
            for z in range(2):
                sy.wait_ge(s_ve, bve + 2 * TT + 5 + z)
                sy.dma_start(out=grow_s[z:z + 1, :], in_=grow[0:1, z, :]).then_inc(s_lds, 16)
        c.add("lds", 32)

        @blk.gpsimd
        def _(g):
            for z in range(2):
                g.wait_ge(s_lds, blds_c + 16 * (z + 1))
                g.dma_start(out=gbc[:, z, :],
                            in_=grow_s[z:z + 1, :].to_broadcast([128, S])).then_inc(s_ldg, 16)
        c.add("ldg", 32)

    if CUT == "b1c":
        return _finish(nc, c, ctx, [ctx_pre, ctx_att], s_lds, s_ldg, s_out, _ncd)

    ctx_pre.close()

    wdq = asb("wdq", [128, 2, QC, 128], BF16)
    wro = asb("wro", [128, 2, QC, 256], BF16)
    wdk = asb("wdk", [128, 2, KC, 128], BF16)
    wdv = asb("wdv", [128, 2, KC, 128], BF16)
    qn_h = asb("qn_h", [128, 2, S], BF16)
    kn_h = asb("kn_h", [128, 2, S], BF16)
    v_h = asb("v_h", [128, 2, S], BF16)
    q_ext = asb("q_ext", [65, 2, S], BF16)
    qev_raw = asb("qev_raw", [128, S], BF16)
    qod_raw = asb("qod_raw", [128, S], BF16)
    qev_rot = asb("qev_rot", [128, S], BF16)
    qod_rot = asb("qod_rot", [128, S], BF16)
    bsc1 = asb("bsc1", [128, S], BF16)
    bsc2 = asb("bsc2", [128, S], BF16)
    pt_sb = asb("pt_sb", [128, 4, 512], BF16)
    o_f32 = asb("o_f32", [128, 2, 512], F32)
    o_bf = asb("o_bf", [128, 2, 512], BF16)
    rl_row = asb("rl_row", [1, 2, 512], F32)
    rl_bc = asb("rl_bc", [128, 2, 512], F32)
    mneg16 = asb("mneg16", [128, 16], F32)
    mneg16b = asb("mneg16b", [128, 16], BF16)

    def head_weight_loads(g, hh):
        b = hh % 2
        g.dma_start(out=wdq[:, b, :, :],
                    in_=W_dq_nope[:, hh * 128:(hh + 1) * 128].rearrange("(c p) m -> p c m", p=128)).then_inc(s_ldg, 16)
        g.dma_start(out=wdk[:, b, :, :],
                    in_=W_dk_nope[:, hh * 128:(hh + 1) * 128].rearrange("(c p) m -> p c m", p=128)).then_inc(s_ldg, 16)
        g.dma_start(out=wdv[:, b, :, :],
                    in_=W_dv[:, hh * 128:(hh + 1) * 128].rearrange("(c p) m -> p c m", p=128)).then_inc(s_ldg, 16)
        n = 3 * 16
        if hh % 4 == 0:
            gg = hh // 4
            g.dma_start(out=wro[:, (gg % 2), :, :],
                        in_=W_dq_rope[:, gg * 256:(gg + 1) * 256].rearrange(
                            "(c p) m -> p c m", p=128)).then_inc(s_ldg, 16)
            n += 16
        return n

    def head_weight_scale(v, hh):
        b = hh % 2
        last = None
        for qrc in range(QC):
            last = v.tensor_scalar(wdq[:, b, qrc, :], wdq[:, b, qrc, :], qscol[:, qrc:qrc + 1], None, ALU.mult)
        for kc in range(KC):
            last = v.tensor_scalar(wdk[:, b, kc, :], wdk[:, b, kc, :], kvscol[:, kc:kc + 1], None, ALU.mult)
            last = v.tensor_scalar(wdv[:, b, kc, :], wdv[:, b, kc, :], kvscol[:, kc:kc + 1], None, ALU.mult)
        if hh % 4 == 0:
            gg = hh // 4
            for qrc in range(QC):
                last = v.tensor_scalar(wro[:, gg % 2, qrc, :], wro[:, gg % 2, qrc, :],
                                       qscol[:, qrc:qrc + 1], None, ALU.mult)
        return last

    spans_nkt = [4 * sp + 4 for sp in range(SP)]
    NKT = sum(spans_nkt)
    LOOK = 2

    def emission_idx(nkt):
        st_i, pv_i = {}, {}
        n = 0
        for kt in range(nkt):
            n += 1; st_i[kt] = n
            if kt >= LOOK:
                n += 1; pv_i[kt - LOOK] = n
        for kt in range(max(0, nkt - LOOK), nkt):
            n += 1; pv_i[kt] = n
        return st_i, pv_i

    p1_groups = []
    for qt in range(TT):
        for ksp in range((128 * (qt + 1) + 511) // 512):
            p1_groups.append((qt, ksp))
    NG1 = len(p1_groups)

    NHEADS = CUT_HEADS if CUT_HEADS is not None else NH
    for k in range(NHEADS + 1):
        h = k
        hp = k - 1
        has_p1 = k < NHEADS
        has_p2 = k >= 1
        b = h % 2
        bp = hp % 2
        j = h % 4
        jobs = []
        if has_p1:
            jobs = [("qn", sp) for sp in range(SP)]
            if j == 0:
                jobs += [("qev", sp) for sp in range(SP)]
                jobs += [("qod", sp) for sp in range(SP)]
            jobs += [("kn", sp) for sp in range(SP)]
            jobs += [("v", tt) for tt in range(TT)]
        nproj = len(jobs)
        NQK = nproj - TT if has_p1 else 0
        nw = 0
        if k == 0:
            nw += 48 + 16
        if has_p1 and h + 1 < NH:
            nw += 48 + (16 if (h + 1) % 4 == 0 else 0)

        with nc.Block() as blk, ExitStack() as lc:
            pq = [lc.enter_context(nc.psum_tensor("pq_%d_%d" % (k, i), [128, 512], F32))
                  for i in range(3)]
            if has_p2:
                ppt = [lc.enter_context(nc.psum_tensor("ppt_%d_%d" % (k, i), [128, 512], F32))
                       for i in range(3)]
                po = lc.enter_context(nc.psum_tensor("po_%d" % k, [128, 512], F32))
                pl = lc.enter_context(nc.psum_tensor("pl_%d" % k, [33, 512], F32))
            bmm, bve, bac, bldg, blds = (c.get("mm"), c.get("ve"), c.get("ac"),
                                         c.get("ldg"), c.get("lds"))
            te_b = [bmm]
            for sp in range(SP):
                te_b.append(te_b[-1] + 2 * spans_nkt[sp])
            mP = bmm + (2 * NKT if has_p2 else 0)
            mB = mP + nproj
            ac_b = [bac]
            for sp in range(SP):
                ac_b.append(ac_b[-1] + spans_nkt[sp] + 1)
            aP = bac + ((NKT + SP) if has_p2 else 0)
            evM, evR, evN = {}, {}, {}
            _vec = bve
            for sp in range(SP):
                evM[sp] = []
                for d in range(4):
                    _vec += 1
                    evM[sp].append(_vec)
                _vec += 1
                evR[sp] = _vec
                if sp >= 1:
                    _vec += 1
                    evN[sp - 1] = _vec
            _vec += 1
            evN[SP - 1] = _vec
            vP = bve + (6 * SP if has_p2 else 0)
            SCL0 = 1 if k == 0 else 0
            ASM = vP + SCL0 + 1
            MNB = ASM + NG1 + 1
            gW = bldg + (SP * 16 if has_p2 else 0)

            @blk.gpsimd
            def _(g, k=k, h=h, bldg=bldg, blds=blds, has_p1=has_p1, has_p2=has_p2):
                if has_p2:
                    for sp in range(SP):
                        g.wait_ge(s_lds, blds + 32 * sp + 16)
                        g.dma_start(out=rl_bc[:, sp % 2, :],
                                    in_=rl_s[sp % 2:sp % 2 + 1, :].to_broadcast([128, 512])).then_inc(s_ldg, 16)
                if k == 0:
                    head_weight_loads(g, 0)
                if has_p1 and h + 1 < NH:
                    head_weight_loads(g, h + 1)
                if has_p1:
                    rw1 = D // NH
                    g.dma_start(out=w1b_s[h * rw1:(h + 1) * rw1, :],
                                in_=W1[h * rw1:(h + 1) * rw1, :]).then_inc(s_ldg, 16)
                    rw2 = DFF // NH
                    g.dma_start(out=w2b_s[h * rw2:(h + 1) * rw2, :],
                                in_=W2[h * rw2:(h + 1) * rw2, :]).then_inc(s_ldg, 16)
                    rwo = (NH * VD) // NH
                    g.dma_start(out=wob_s[h * rwo:(h + 1) * rwo, :],
                                in_=W_o[h * rwo:(h + 1) * rwo, :]).then_inc(s_ldg, 16)
            c.add("ldg", (SP * 16 if has_p2 else 0) + nw + (48 if has_p1 else 0))

            if has_p2:
                @blk.sync
                def _(sy, hp=hp, bldg=bldg, blds=blds, evR=evR, evN=evN):
                    sy.wait_ge(s_ldg, bldg)
                    for sp in range(SP):
                        if sp > 1:
                            sy.wait_ge(s_ldg, bldg + 16 * (sp - 1))
                        sy.wait_ge(s_ve, evR[sp])
                        sy.dma_start(out=rl_s[sp % 2:sp % 2 + 1, :],
                                     in_=rl_row[0:1, sp % 2, :]).then_inc(s_lds, 16)
                        sy.wait_ge(s_ve, evN[sp])
                        srcap = o_bf[:, sp % 2, :].rearrange("p (tt ti) -> p tt ti", ti=128)
                        dstap = o_s[4 * sp:4 * sp + 4, hp, :, :].transpose([1, 0, 2])
                        sy.dma_start(out=dstap, in_=srcap).then_inc(s_lds, 16)
                c.add("lds", SP * 32)

            @blk.tensor
            def _(t, k=k, b=b, bp=bp, jobs=jobs, te_b=te_b, ac_b=ac_b, evM=evM, evR=evR,
                  mP=mP, mB=mB, aP=aP, vP=vP, ASM=ASM, MNB=MNB, SCL0=SCL0,
                  has_p1=has_p1, has_p2=has_p2, NQK=NQK):
                if has_p2:
                    def emit_st(sp, kt):
                        if kt >= 3:
                            t.wait_ge(s_ac, ac_b[sp] + (kt - 3) + 1)
                        t.matmul(ppt[kt % 3][:], kn_h[:, bp, kt * 128:(kt + 1) * 128],
                                 qn_h[:, bp, sp * 512:(sp + 1) * 512], start=True, stop=False)
                        t.matmul(ppt[kt % 3][:], kx_rope[:, kt * 128:(kt + 1) * 128],
                                 q_ext[:, bp, sp * 512:(sp + 1) * 512], start=False, stop=True).then_inc(s_mm, 1)

                    def emit_pv(sp, kt, nkt):
                        t.wait_ge(s_ac, ac_b[sp] + kt + 1)
                        if kt >= 4 * sp:
                            t.wait_ge(s_ve, evM[sp][kt - 4 * sp])
                        if kt == 0 and sp >= 1:
                            t.wait_ge(s_ac, ac_b[sp])
                        if kt == 0 and sp > 1:
                            t.wait_ge(s_ve, evR[sp - 2])
                        plr = 32 * (sp % 2)
                        t.matmul(po[:], v_h[:, bp, kt * 128:(kt + 1) * 128], pt_sb[:, kt % 4, :],
                                 start=(kt == 0), stop=(kt == nkt - 1))
                        t.matmul(pl[plr:plr + 1, :], ones_col[:], pt_sb[:, kt % 4, :],
                                 start=(kt == 0), stop=(kt == nkt - 1)).then_inc(s_mm, 1)

                    for sp in range(SP):
                        nkt = spans_nkt[sp]
                        for kt in range(nkt):
                            emit_st(sp, kt)
                            if kt >= LOOK:
                                emit_pv(sp, kt - LOOK, nkt)
                        for kt in range(max(0, nkt - LOOK), nkt):
                            emit_pv(sp, kt, nkt)
                if has_p1:
                    for i, (kind, idx) in enumerate(jobs):
                        if i >= 3:
                            t.wait_ge(s_ac, aP + (i - 3) + 1)
                        if k == 0 and i == 0:
                            t.wait_ge(s_ve, vP + 1)
                        if kind == "qn":
                            for qrc in range(QC):
                                last = t.matmul(pq[i % 3][:], wdq[:, b, qrc, :],
                                                cq_bf[:, qrc, idx * 512:(idx + 1) * 512],
                                                start=(qrc == 0), stop=(qrc == QC - 1))
                        elif kind in ("qev", "qod"):
                            two = 0 if kind == "qev" else 1
                            for qrc in range(QC):
                                wv = wro[:, (h // 4) % 2, qrc, :].rearrange(
                                    "p (hh jj two) -> p hh jj two", hh=4, two=2)[:, :, :, two]
                                last = t.matmul(pq[i % 3][:], wv,
                                                cq_bf[:, qrc, idx * 512:(idx + 1) * 512],
                                                start=(qrc == 0), stop=(qrc == QC - 1))
                        elif kind == "kn":
                            for kc in range(KC):
                                last = t.matmul(pq[i % 3][:], wdk[:, b, kc, :],
                                                ckv_bf[:, kc, idx * 512:(idx + 1) * 512],
                                                start=(kc == 0), stop=(kc == KC - 1))
                        else:
                            for kc in range(KC):
                                last = t.matmul(pq[i % 3][:, 0:128], ckv_bf[:, kc, idx * 128:(idx + 1) * 128],
                                                wdv[:, b, kc, :],
                                                start=(kc == 0), stop=(kc == KC - 1))
                        last.then_inc(s_mm, 1)
                    t.wait_ge(s_ac, aP + NQK)
                    t.wait_ge(s_ve, ASM)
                    last_proj = {bi: max(i for i in range(nproj) if i % 3 == bi) for bi in range(3)}
                    last_reader = {0: None, 1: None, 2: None}
                    for gi, (qt, ksp) in enumerate(p1_groups):
                        bi = gi % 3
                        if last_reader[bi] is None:
                            t.wait_ge(s_ac, aP + last_proj[bi] + 1)
                        else:
                            t.wait_ge(s_ve, ASM + last_reader[bi] + 1)
                        n = min(512, 128 * (qt + 1) - ksp * 512)
                        dst = pq[bi][:, 0:n]
                        t.matmul(dst, qn_h[:, b, qt * 128:(qt + 1) * 128],
                                 kn_h[:, b, ksp * 512:ksp * 512 + n], start=True, stop=False)
                        t.matmul(dst, q_ext[0:64, b, qt * 128:(qt + 1) * 128],
                                 kx_rope[0:64, ksp * 512:ksp * 512 + n], start=False, stop=True).then_inc(s_mm, 1)
                        last_reader[bi] = gi
                    t.wait_ge(s_ve, MNB)
                    for sp in range(SP):
                        if sp >= 1:
                            t.wait_ge(s_ve, MNB + sp)
                        for j2 in range(4):
                            qt = 4 * sp + j2
                            last = t.matmul(pq[0][0:1, j2 * 128:(j2 + 1) * 128], mneg16b[:, qt:qt + 1],
                                            idbf[:], start=True, stop=True)
                        last.then_inc(s_mm, 1)
            c.add("mm", (2 * NKT if has_p2 else 0) + ((nproj + NG1 + SP) if has_p1 else 0))

            @blk.scalar
            def _(sc, jobs=jobs, b=b, te_b=te_b, evN=evN, mP=mP,
                  has_p1=has_p1, has_p2=has_p2):
                if has_p2:
                    for sp in range(SP):
                        nkt = spans_nkt[sp]
                        st_i, pv_i = emission_idx(nkt)
                        for kt in range(nkt):
                            sc.wait_ge(s_mm, te_b[sp] + st_i[kt])
                            if kt >= 4:
                                sc.wait_ge(s_mm, te_b[sp] + pv_i[kt - 4])
                            sc.activation(pt_sb[:, kt % 4, :], ppt[kt % 3][:], AF.Exp).then_inc(s_ac, 1)
                        sc.wait_ge(s_mm, te_b[sp] + 2 * nkt)
                        if sp >= 2:
                            sc.wait_ge(s_ve, evN[sp - 2])
                        sc.activation(o_f32[:, sp % 2, :], po[:], AF.Identity).then_inc(s_ac, 1)
                if has_p1:
                    for i, (kind, idx) in enumerate(jobs):
                        sc.wait_ge(s_mm, mP + i + 1)
                        if kind == "qn":
                            dst, srcp = qn_h[:, b, idx * 512:(idx + 1) * 512], pq[i % 3][:]
                        elif kind == "qev":
                            dst, srcp = qev_raw[:, idx * 512:(idx + 1) * 512], pq[i % 3][:]
                        elif kind == "qod":
                            dst, srcp = qod_raw[:, idx * 512:(idx + 1) * 512], pq[i % 3][:]
                        elif kind == "kn":
                            dst, srcp = kn_h[:, b, idx * 512:(idx + 1) * 512], pq[i % 3][:]
                        else:
                            dst, srcp = v_h[:, b, idx * 128:(idx + 1) * 128], pq[i % 3][:, 0:128]
                        sc.activation(dst, srcp, AF.Identity).then_inc(s_ac, 1)
            c.add("ac", ((NKT + SP) if has_p2 else 0) + (nproj if has_p1 else 0))

            @blk.vector
            def _(v, k=k, h=h, j=j, b=b, te_b=te_b, ac_b=ac_b,
                  mB=mB, aP=aP, vP=vP, ASM=ASM, SCL0=SCL0, gW=gW, nw=nw,
                  bldg=bldg, blds=blds, has_p1=has_p1, has_p2=has_p2):
                v = VG(v)
                if has_p2:
                    def emit_norm(sp):
                        v.wait_ge(s_ldg, bldg + 16 * (sp + 1))
                        v.wait_ge(s_ac, ac_b[sp] + spans_nkt[sp] + 1)
                        if sp > 1:
                            v.wait_ge(s_lds, blds + 32 * (sp - 1))
                        v.tensor_tensor(o_bf[:, sp % 2, :], o_f32[:, sp % 2, :],
                                        rl_bc[:, sp % 2, :], ALU.mult).then_inc(s_ve, 1)

                    for sp in range(SP):
                        nkt = spans_nkt[sp]
                        for d in range(4):
                            kt = 4 * sp + d
                            v.wait_ge(s_ac, ac_b[sp] + kt + 1)
                            v.tensor_tensor(pt_sb[:, kt % 4, :], pt_sb[:, kt % 4, :],
                                            dmask_sb[:, d, :], ALU.mult).then_inc(s_ve, 1)
                        v.wait_ge(s_mm, te_b[sp] + 2 * nkt)
                        if sp > 1:
                            v.wait_ge(s_lds, blds + 32 * (sp - 2) + 16)
                        v.reciprocal(rl_row[0:1, sp % 2, :],
                                     pl[32 * (sp % 2):32 * (sp % 2) + 1, :]).then_inc(s_ve, 1)
                        if sp >= 1:
                            emit_norm(sp - 1)
                    emit_norm(SP - 1)
                if has_p1:
                    if k == 0:
                        v.wait_ge(s_ldg, gW + 64)
                        head_weight_scale(v, 0).then_inc(s_ve, 1)
                    if j == 0:
                        v.wait_ge(s_ac, aP + 3 * SP)
                        v.tensor_tensor(bsc1[:], qev_raw[:], tabs[:, 0, :], ALU.mult)
                        v.tensor_tensor(bsc2[:], qod_raw[:], tabs[:, 1, :], ALU.mult)
                        v.tensor_tensor(qev_rot[:], bsc1[:], bsc2[:], ALU.subtract)
                        v.tensor_tensor(bsc1[:], qev_raw[:], tabs[:, 1, :], ALU.mult)
                        v.tensor_tensor(bsc2[:], qod_raw[:], tabs[:, 0, :], ALU.mult)
                        v.tensor_tensor(qod_rot[:], bsc1[:], bsc2[:], ALU.add)
                    v.tensor_copy(out=q_ext[0:32, b, :], in_=qev_rot[32 * j:32 * j + 32, :])
                    v.tensor_copy(out=q_ext[32:64, b, :], in_=qod_rot[32 * j:32 * j + 32, :]).then_inc(s_ve, 1)
                    for gi, (qt, ksp) in enumerate(p1_groups):
                        v.wait_ge(s_mm, mB + gi + 1)
                        n = min(512, 128 * (qt + 1) - ksp * 512)
                        bi = gi % 3
                        if ksp == 0:
                            last = v.reduce_max(small[:, 5:6], pq[bi][:, 0:n], axis=AX.X)
                        else:
                            v.reduce_max(small[:, 6:7], pq[bi][:, 0:n], axis=AX.X)
                            last = v.tensor_max(small[:, 5:6], small[:, 5:6], small[:, 6:7])
                        if gi == NG1 - 1 or p1_groups[gi + 1][0] != qt:
                            last = v.tensor_scalar(mneg16[:, qt:qt + 1], small[:, 5:6], -1.0,
                                                   None, ALU.mult)
                        last.then_inc(s_ve, 1)
                    v.tensor_copy(out=mneg16b[:, 0:TT], in_=mneg16[:, 0:TT]).then_inc(s_ve, 1)
                    for sp in range(SP):
                        v.wait_ge(s_mm, mB + NG1 + sp + 1)
                        v.tensor_copy(out=q_ext[64:65, b, sp * 512:(sp + 1) * 512],
                                      in_=pq[0][0:1, :]).then_inc(s_ve, 1)
                    if h + 1 < NH:
                        v.wait_ge(s_ldg, gW + nw)
                        head_weight_scale(v, h + 1)
            c.add("ve", (6 * SP if has_p2 else 0) +
                  ((SCL0 + 1 + NG1 + 1 + SP) if has_p1 else 0))

    if CUT == "b2":
        return _finish(nc, c, ctx, [ctx_att], s_lds, s_ldg, s_out, _ncd)

    ctx_att.close()

    ctx_ffn = ExitStack()
    fsb = lambda n, shp, dt=F32: ctx_ffn.enter_context(nc.sbuf_tensor(n, shp, dt))
    wr_bc = fsb("wr_bc", [128, D], F32)
    br_col = fsb("br_col", [128, 1], F32)

    with nc.Block() as blk, ExitStack() as lc:
        lsb = lambda n, shp, dt=F32: lc.enter_context(nc.sbuf_tensor(n, shp, dt))
        w_o_bf = lsb("w_o_bf", [128, NH, D], BF16)
        o_tile = lsb("o_tile", [128, 2, NH * 128], BF16)
        x0t = lsb("x0t3", [128, 2, D], F32)
        a_sb = lsb("a_sb", [128, 2, D], F32)
        acc4 = lsb("acc4", [128, 2, 8], F32)
        x2t = lsb("x2t", [128, 2, D], F32)
        y32 = lsb("y32", [128, D], F32)
        statb = lsb("statb", [128, 2, 4], F32)
        bn2 = lsb("bn23", [128, 2], F32)
        rc1 = lsb("rc1", [128, 1], F32)
        mr1 = lsb("mr1", [128, 1], F32)
        p_a = [lc.enter_context(nc.psum_tensor("p_a%d" % k, [128, 512], F32))
               for k in range(8)]
        bmm, bve, bac, blds, bldg = (c.get("mm"), c.get("ve"), c.get("ac"),
                                     c.get("lds"), c.get("ldg"))
        NA = DSP + 5
        NV = 5

        def posH(tt):
            return 0 if tt == 0 else (DSP + 1) + NA * (tt - 1)

        def posT(tt):
            return 2 * (DSP + 1) + NA * tt if tt <= TT - 2 else NA * TT - 4

        @blk.gpsimd
        def _(g):
            g.dma_start(out=wr_bc[:],
                        in_=W_r.rearrange("d u -> u d").to_broadcast([128, D])).then_inc(s_ldg, 16)
            g.dma_start(out=br_col[:],
                        in_=b_r.unsqueeze(0).to_broadcast([128, 1])).then_inc(s_ldg, 16)
            g.dma_start(out=w_o_bf[:, 0:NH // 2, :],
                        in_=wob_s[0:(NH // 2) * 128, :].rearrange("(c p) d -> p c d", p=128)).then_inc(s_ldg, 16)
            for tt in range(TT):
                g.wait_ge(s_ve, bve + NV * tt + 2)
                g.dma_start(out=x2_s[tt * 128:(tt + 1) * 128, :],
                            in_=x2t[:, tt % 2, :]).then_inc(s_ldg, 16)
                g.wait_ge(s_ve, bve + NV * tt + 4)
                g.dma_start(out=y_s[tt * 128:(tt + 1) * 128, :], in_=y32[:]).then_inc(s_ldg, 16)
        c.add("ldg", 48 + TT * 32)

        @blk.sync
        def _(sy):
            sy.wait_ge(s_ldg, bldg)
            sy.dma_start(out=w_o_bf[:, NH // 2:NH, :],
                         in_=wob_s[(NH // 2) * 128:NH * 128, :].rearrange("(c p) d -> p c d", p=128)).then_inc(s_lds, 16)
            sy.wait_ge(s_lds, blds)
            for tt in range(TT):
                if tt >= 2:
                    sy.wait_ge(s_mm, bmm + (tt - 1) * DSP)
                sy.dma_start(out=o_tile[:, tt % 2, :].rearrange("v (h t) -> v h t", t=128),
                             in_=o_s[tt].transpose([1, 0, 2])).then_inc(s_lds, 16)
                if tt >= 2:
                    sy.wait_ge(s_ve, bve + NV * (tt - 2) + 2)
                sy.dma_start(out=x0t[:, tt % 2, :], in_=x_in[tt * 128:(tt + 1) * 128, :]).then_inc(s_lds, 16)
                if DBG_SCRATCH and tt == 1:
                    sy.wait_ge(s_ve, bve + NV * tt + 2)
                    sy.dma_start(out=adbg[:], in_=a_sb[:, tt % 2, :]).then_inc(s_out, 16)
        c.add("lds", 16 + TT * 2 * 16)
        if DBG_SCRATCH:
            c.add("out", 16)

        @blk.tensor
        def _(t):
            t.wait_ge(s_ldg, bldg + 48)
            t.wait_ge(s_lds, blds + 16)
            for tt in range(TT):
                t.wait_ge(s_lds, blds + 16 + tt * 32 + 16)
                for dsp in range(DSP):
                    i = tt * DSP + dsp
                    if i >= 8:
                        j = i - 8
                        t.wait_ge(s_ac, bac + posH(j // DSP) + (j % DSP) + 1)
                    for hh in range(NH):
                        last = t.matmul(p_a[i % 8][:], o_tile[:, tt % 2, hh * 128:(hh + 1) * 128],
                                        w_o_bf[:, hh, dsp * 512:(dsp + 1) * 512],
                                        start=(hh == 0), stop=(hh == NH - 1))
                    last.then_inc(s_mm, 1)
        c.add("mm", TT * DSP)

        @blk.scalar
        def _(sc):

            def emit_H(tt):
                if tt >= 2:
                    sc.wait_ge(s_ve, bve + NV * (tt - 2) + 5)
                for dsp in range(DSP):
                    sc.wait_ge(s_mm, bmm + tt * DSP + dsp + 1)
                    sc.activation(a_sb[:, tt % 2, dsp * 512:(dsp + 1) * 512],
                                  p_a[(tt * DSP + dsp) % 8][:], AF.Identity,
                                  accum_out=acc4[:, tt % 2, dsp:dsp + 1]).then_inc(s_ac, 1)
                if tt >= 2:
                    sc.wait_ge(s_ve, bve + NV * (tt - 2) + 4)
                    sc.wait_ge(s_ldg, bldg + 48 + 32 * (tt - 2) + 16)
                sc.activation(x2t[:, tt % 2, :], a_sb[:, tt % 2, :], AF.Square,
                              accum_out=acc4[:, tt % 2, DSP:DSP + 1]).then_inc(s_ac, 1)

            def emit_T(tt):
                sc.wait_ge(s_ve, bve + NV * tt + 1)
                sc.activation(statb[:, tt % 2, 1:2], statb[:, tt % 2, 0:1], AF.Sqrt).then_inc(s_ac, 1)
                sc.wait_ge(s_ve, bve + NV * tt + 2)
                sc.activation(a_sb[:, tt % 2, :], x2t[:, tt % 2, :], AF.Square,
                              accum_out=acc4[:, tt % 2, DSP + 1:DSP + 2]).then_inc(s_ac, 1)
                sc.activation(a_sb[:, tt % 2, :], x2t[:, tt % 2, :], AF.Identity,
                              accum_out=acc4[:, tt % 2, DSP + 2:DSP + 3]).then_inc(s_ac, 1)
                sc.wait_ge(s_ve, bve + NV * tt + 3)
                sc.activation(statb[:, tt % 2, 3:4], statb[:, tt % 2, 2:3], AF.Sqrt).then_inc(s_ac, 1)

            for tt in range(TT):
                emit_H(tt)
                if tt >= 1:
                    emit_T(tt - 1)
            emit_T(TT - 1)
        c.add("ac", TT * NA)

        @blk.vector
        def _(v):
            v = VG(v)
            for tt in range(TT):
                v.wait_ge(s_ac, bac + posH(tt) + DSP + 1)
                a4 = acc4[:, tt % 2, :]
                if DSP == 4:
                    v.tensor_add(small[:, 4:5], a4[:, 0:1], a4[:, 1:2])
                    v.tensor_add(small[:, 5:6], a4[:, 2:3], a4[:, 3:4])
                    v.tensor_add(small[:, 4:5], small[:, 4:5], small[:, 5:6])
                elif DSP == 2:
                    v.tensor_add(small[:, 4:5], a4[:, 0:1], a4[:, 1:2])
                else:
                    v.tensor_copy(out=small[:, 4:5], in_=a4[:, 0:1])
                v.tensor_scalar(bn2[:, 0:1], small[:, 4:5], 1.0 / D, None, ALU.mult)
                v.tensor_scalar(bn2[:, 1:2], a4[:, DSP:DSP + 1], 1.0 / D, None, ALU.mult)
                v.tensor_mul(small[:, 3:4], bn2[:, 0:1], bn2[:, 0:1])
                v.tensor_tensor(bn2[:, 1:2], bn2[:, 1:2], small[:, 3:4], ALU.subtract)
                v.tensor_scalar(statb[:, tt % 2, 0:1], bn2[:, 1:2], float(LN_EPS), None, ALU.add).then_inc(s_ve, 1)
                v.wait_ge(s_ac, bac + posT(tt) + 1)
                v_rsqrt(v, statb[:, tt % 2, 0:1], statb[:, tt % 2, 1:2], rc1[:], small[:, 2:3])
                v.tensor_mul(mr1[:], bn2[:, 0:1], rc1[:])
                v.tensor_scalar(mr1[:], mr1[:], -1.0, None, ALU.mult)
                v.wait_ge(s_lds, blds + 16 + 32 * tt + 32)
                v.scalar_tensor_tensor(x2t[:, tt % 2, :], a_sb[:, tt % 2, :], rc1[:],
                                       x0t[:, tt % 2, :], ALU.mult, ALU.add)
                v.tensor_scalar(x2t[:, tt % 2, :], x2t[:, tt % 2, :], mr1[:], None, ALU.add).then_inc(s_ve, 1)
                v.wait_ge(s_ac, bac + posT(tt) + 3)
                v.tensor_scalar(bn2[:, 0:1], a4[:, DSP + 2:DSP + 3], 1.0 / D, None, ALU.mult)
                v.tensor_scalar(bn2[:, 1:2], a4[:, DSP + 1:DSP + 2], 1.0 / D, None, ALU.mult)
                v.tensor_mul(small[:, 3:4], bn2[:, 0:1], bn2[:, 0:1])
                v.tensor_tensor(bn2[:, 1:2], bn2[:, 1:2], small[:, 3:4], ALU.subtract)
                v.tensor_scalar(statb[:, tt % 2, 2:3], bn2[:, 1:2], float(LN_EPS), None, ALU.add).then_inc(s_ve, 1)
                v.wait_ge(s_ac, bac + posT(tt) + 4)
                v_rsqrt(v, statb[:, tt % 2, 2:3], statb[:, tt % 2, 3:4], rc1[:], small[:, 2:3])
                v.tensor_copy(out=r2col[:, tt:tt + 1], in_=rc1[:])
                v.tensor_mul(mr1[:], bn2[:, 0:1], rc1[:])
                v.tensor_scalar(mr1[:], mr1[:], -1.0, None, ALU.mult)
                v.tensor_copy(out=m2r2[:, tt:tt + 1], in_=mr1[:])
                if tt >= 1:
                    v.wait_ge(s_ldg, bldg + 48 + 32 * (tt - 1) + 32)
                v.tensor_scalar(y32[:], x2t[:, tt % 2, :], rc1[:], mr1[:],
                                ALU.mult, ALU.add).then_inc(s_ve, 1)
                if tt == 0:
                    v.wait_ge(s_ldg, bldg + 32)
                v.scalar_tensor_tensor(a_sb[:, tt % 2, :], y32[:], 1.0, wr_bc[:],
                                       ALU.mult, ALU.mult, accum_out=small[:, 6:7])
                v.tensor_tensor(w16[:, tt:tt + 1], small[:, 6:7], br_col[:], ALU.add).then_inc(s_ve, 1)
        c.add("ve", TT * NV)

    if CUT == "b3":
        return _finish(nc, c, ctx, [ctx_ffn], s_lds, s_ldg, s_out, _ncd)

    W_all = fsb("W_all", [128, S], F32)
    rowC = fsb("rowC", [1, S], F32)
    wg_bc = fsb("wg_bc", [128, KSEL], F32)
    rt_bc = fsb("rt_bc", [128, S], F32)
    rt16 = fsb("rt16", [128, TT], F32)
    wselc = fsb("wselc", [128, SC], F32)
    P_sb = fsb("P_sb", [128, TT, KSEL], BF16)
    PT_sb = fsb("PT_sb", [128, SC, S], BF16)
    b1cols = fsb("b1cols", [128, FC], F32)
    b2cols = fsb("b2cols", [128, DC], F32)

    with nc.Block() as blk, ExitStack() as lc:
        scr2 = lc.enter_context(nc.sbuf_tensor("scr2", [128, S], F32))
        cnt16 = lc.enter_context(nc.sbuf_tensor("cnt16", [128, TT], F32))
        blds, bldg, bve = c.get("lds"), c.get("ldg"), c.get("ve")

        @blk.sync
        def _(sy):
            sy.dma_start(out=wrow_s.squeeze(0).rearrange("(cc p) -> p cc", p=128),
                         in_=w16[:]).then_inc(s_lds, 16)
        c.add("lds", 16)

        @blk.gpsimd
        def _(g):
            g.wait_ge(s_lds, blds + 16)
            g.dma_start(out=W_all[:], in_=wrow_s.to_broadcast([128, S])).then_inc(s_ldg, 16)
        c.add("ldg", 16)

        @blk.vector
        def _(v):
            v = VG(v)
            v.wait_ge(s_ldg, bldg + 16)
            for tc in range(TT):
                v._v.tensor_scalar(scr2[:], W_all[:], w16[:, tc:tc + 1], None,
                                   ALU.is_ge, ALU.add, accum_out=cnt16[:, tc:tc + 1])
            v.drain()
            v.tensor_single_scalar(selc[:], cnt16[:], KSEL - 0.5, ALU.is_le)
            v.tensor_scalar(nselc[:], selc[:], -1.0, 1.0, ALU.mult, ALU.add)
        c.add("ve", 0)

    if CUT == "b4a":
        return _finish(nc, c, ctx, [ctx_ffn], s_lds, s_ldg, s_out, _ncd)

    with nc.Block() as blk:
        blds, bve, bldg = c.get("lds"), c.get("ve"), c.get("ldg")

        @blk.sync
        def _(sy):
            sy.dma_start(out=selrow_s.squeeze(0).rearrange("(cc p) -> p cc", p=128),
                         in_=selc[:]).then_inc(s_lds, 16)
            sy.wait_ge(s_lds, blds + 16)
            sy.dma_start(out=rowB[0:1, 0:S], in_=selrow_s[:]).then_inc(s_lds, 16)
            sy.wait_ge(s_ve, bve + 1)
            sy.dma_start(out=rt_s[:], in_=rowA[0:1, 0:S]).then_inc(s_lds, 16)
            sy.wait_ge(s_lds, blds + 48)
            sy.dma_start(out=rt16[:], in_=rt_s.squeeze(0).rearrange("(cc p) -> p cc", p=128)).then_inc(s_lds, 16)
        c.add("lds", 64)

        @blk.vector
        def _(v):
            v = VG(v)
            v.memset(rowC[:], 1.0)
            v.wait_ge(s_lds, blds + 32)
            v.tensor_tensor_scan(rowA[0:1, :], rowC[0:1, :], rowB[0:1, :], 0.0,
                                 ALU.mult, ALU.add)
            v.tensor_tensor(rowA[0:1, :], rowA[0:1, :], rowB[0:1, :], ALU.mult)
            v.tensor_scalar(rowA[0:1, :], rowA[0:1, :], -1.0, None, ALU.add).then_inc(s_ve, 1)
            v.wait_ge(s_ldg, bldg + 16)
            for sc_i in range(SC):
                v._v.tensor_scalar(PT_sb[:, sc_i, :], rt_bc[:], iotacols_sb[:, sc_i:sc_i + 1],
                                   None, ALU.is_equal)
            v.wait_ge(s_lds, blds + 64)
            for tc in range(TT):
                last = v._v.tensor_scalar(P_sb[:, tc, :], iotak_sb[:], rt16[:, tc:tc + 1],
                                          None, ALU.is_equal)
            last.then_inc(s_ve, 1)
        c.add("ve", 2)

        @blk.gpsimd
        def _(g):
            g.wait_ge(s_lds, blds + 48)
            g.dma_start(out=rt_bc[:], in_=rt_s.to_broadcast([128, S])).then_inc(s_ldg, 16)
        c.add("ldg", 16)

    if CUT == "b4b":
        return _finish(nc, c, ctx, [ctx_ffn], s_lds, s_ldg, s_out, _ncd)

    with nc.Block() as blk, ExitStack() as lc:
        scr2 = lc.enter_context(nc.sbuf_tensor("scr2b", [128, S], F32))
        blds, bve, bldg = c.get("lds"), c.get("ve"), c.get("ldg")

        @blk.vector
        def _(v):
            v = VG(v)
            for sc_i in range(SC):
                v._v.scalar_tensor_tensor(scr2[:], rt_bc[:], iotacols_sb[:, sc_i:sc_i + 1],
                                          W_all[:], ALU.is_equal, ALU.mult,
                                          accum_out=wselc[:, sc_i:sc_i + 1])
            v.drain()
            v._v.tensor_scalar(small[:, 7:8], wselc[:, 0:1], 1.0, None,
                               ALU.mult).then_inc(s_ve, 1)
        c.add("ve", 1)

        @blk.sync
        def _(sy):
            sy.wait_ge(s_ve, bve + 1)
            sy.dma_start(out=wselrow_s.squeeze(0).rearrange("(cc p) -> p cc", p=128),
                         in_=wselc[:]).then_inc(s_lds, 16)
            sy.dma_start(out=b1cols[:], in_=b1.rearrange("(c p) -> p c", p=128)).then_inc(s_lds, 16)
            sy.dma_start(out=b2cols[:], in_=b2.rearrange("(c p) -> p c", p=128)).then_inc(s_lds, 16)
        c.add("lds", 48)

        @blk.gpsimd
        def _(g):
            g.wait_ge(s_lds, blds + 16)
            g.dma_start(out=wg_bc[:], in_=wselrow_s.to_broadcast([128, KSEL])).then_inc(s_ldg, 16)
        c.add("ldg", 16)

    if CUT == "b4c":
        return _finish(nc, c, ctx, [ctx_ffn], s_lds, s_ldg, s_out, _ncd)

    ctx_ffn2 = ExitStack()
    xg_sb = ctx_ffn2.enter_context(nc.sbuf_tensor("xg_sb", [128, DC, KSEL], BF16))
    with nc.Block() as blk, ExitStack() as lc:
        y_t = lc.enter_context(nc.sbuf_tensor("y_t", [128, TT, D], BF16))
        p_x = [lc.enter_context(nc.psum_tensor("p_x%d" % k, [128, KSEL], F32))
               for k in range(8)]
        blds, bldg, bmm, bac = c.get("lds"), c.get("ldg"), c.get("mm"), c.get("ac")
        DH2 = DC // 2

        @blk.sync
        def _(sy):
            sy.wait_ge(s_ldg, bldg)
            for tc in range(1, TT, 2):
                sy.dma_start(out=y_t[:, tc, :],
                             in_=y_s[tc * 128:(tc + 1) * 128, :]).then_inc(s_lds, 16)
        c.add("lds", (TT // 2) * 16)

        @blk.gpsimd
        def _(g):
            for tc in range(0, TT, 2):
                g.dma_start(out=y_t[:, tc, :],
                            in_=y_s[tc * 128:(tc + 1) * 128, :]).then_inc(s_ldg, 16)
        c.add("ldg", (TT - TT // 2) * 16)

        @blk.tensor
        def _(t):
            for half in range(2):
                for tc in range(TT):
                    if half == 0:
                        if tc % 2 == 0:
                            t.wait_ge(s_ldg, bldg + (tc // 2 + 1) * 16)
                        else:
                            t.wait_ge(s_lds, blds + (tc // 2 + 1) * 16)
                    if tc == 0 and half == 1:
                        t.wait_ge(s_ac, bac + DH2)
                    for dc8 in range(DH2):
                        dc = half * DH2 + dc8
                        last = t.matmul(p_x[dc8][:], y_t[:, tc, dc * 128:(dc + 1) * 128],
                                        P_sb[:, tc, :], start=(tc == 0), stop=(tc == TT - 1))
                    last.then_inc(s_mm, 1)
        c.add("mm", 2 * TT)

        @blk.scalar
        def _(sc):
            for half in range(2):
                sc.wait_ge(s_mm, bmm + (half + 1) * TT)
                for dc8 in range(DH2):
                    dc = half * DH2 + dc8
                    sc.activation(xg_sb[:, dc, :], p_x[dc8][:], AF.Identity).then_inc(s_ac, 1)
        c.add("ac", DC)

    if CUT == "b5a":
        return _finish(nc, c, ctx, [ctx_ffn2, ctx_ffn], s_lds, s_ldg, s_out, _ncd)

    h_sb = ctx_ffn2.enter_context(nc.sbuf_tensor("h_sb", [128, FC, KSEL], BF16))
    with nc.Block() as blk, ExitStack() as lc:
        w1sl = lc.enter_context(nc.sbuf_tensor("w1sl", [128, 2, DC, 2, 128], BF16))
        p_h = [lc.enter_context(nc.psum_tensor("p_h%d" % k, [128, KSEL], F32))
               for k in range(2)]
        bldg, bmm, bac = c.get("ldg"), c.get("mm"), c.get("ac")
        blds = c.get("lds")
        blds_cols = c.get("lds")

        @blk.sync
        def _(sy):
            sy.wait_ge(s_ldg, bldg)
            for fc2 in range(0, FC, 2):
                if fc2 >= 4:
                    sy.wait_ge(s_mm, bmm + fc2 - 2)
                sy.dma_start(out=w1sl[:, (fc2 // 2) % 2, :, :, :],
                             in_=w1b_s[:, fc2 * 128:(fc2 + 2) * 128].rearrange("(c p) (two m) -> p c two m", p=128, two=2)).then_inc(s_lds, 16)
        c.add("lds", (FC // 2) * 16)

        @blk.tensor
        def _(t):
            for fc in range(FC):
                t.wait_ge(s_lds, blds + (fc // 2 + 1) * 16)
                if fc >= 2:
                    t.wait_ge(s_ac, bac + fc - 1)
                for dc in range(DC):
                    last = t.matmul(p_h[fc % 2][:], w1sl[:, (fc // 2) % 2, dc, fc % 2, :], xg_sb[:, dc, :],
                                    start=(dc == 0), stop=(dc == DC - 1))
                last.then_inc(s_mm, 1)
        c.add("mm", FC)

        @blk.scalar
        def _(sc):
            sc.wait_ge(s_lds, blds_cols)
            for fc in range(FC):
                sc.wait_ge(s_mm, bmm + fc + 1)
                sc.activation(h_sb[:, fc, :], p_h[fc % 2][:], AF.Gelu_apprx_tanh,
                              bias=b1cols[:, fc:fc + 1]).then_inc(s_ac, 1)
        c.add("ac", FC)

    if CUT == "b5b":
        return _finish(nc, c, ctx, [ctx_ffn2, ctx_ffn], s_lds, s_ldg, s_out, _ncd)

    with nc.Block() as blk, ExitStack() as lc:
        w2sl = lc.enter_context(nc.sbuf_tensor("w2sl", [128, 4, 2, 1024], BF16))
        vf = lc.enter_context(nc.sbuf_tensor("vf", [128, KSEL], F32))
        vb = lc.enter_context(nc.sbuf_tensor("vb", [128, DC, KSEL], BF16))
        p_v = [lc.enter_context(nc.psum_tensor("p_v%d" % k, [128, KSEL], F32))
               for k in range(8)]
        bldg, bmm, bac, bve, blds = (c.get("ldg"), c.get("mm"), c.get("ac"),
                                     c.get("ve"), c.get("lds"))
        blds_w = blds
        DH2 = DC // 2

        @blk.gpsimd
        def _(g):
            for half in range(2):
                for fc2 in range(0, FC, 2):
                    i = half * (FC // 2) + fc2 // 2
                    if i % 2 == 1:
                        continue
                    if i >= 4:
                        g.wait_ge(s_mm, bmm + 2 * i - 6)
                    g.dma_start(out=w2sl[:, i % 4, :, 0:DH2 * 128],
                                in_=w2b_s[fc2 * 128:(fc2 + 2) * 128,
                                          half * DH2 * 128:(half + 1) * DH2 * 128].rearrange(
                                              "(two p) m -> p two m", two=2)).then_inc(s_ldg, 16)

        @blk.sync
        def _(sy):
            for half in range(2):
                for fc2 in range(0, FC, 2):
                    i = half * (FC // 2) + fc2 // 2
                    if i % 2 == 0:
                        continue
                    if i >= 4:
                        sy.wait_ge(s_mm, bmm + 2 * i - 6)
                    sy.dma_start(out=w2sl[:, i % 4, :, 0:DH2 * 128],
                                 in_=w2b_s[fc2 * 128:(fc2 + 2) * 128,
                                           half * DH2 * 128:(half + 1) * DH2 * 128].rearrange(
                                               "(two p) m -> p two m", two=2)).then_inc(s_lds, 16)
        _ng = sum(1 for _h in range(2) for _f in range(0, FC, 2)
                  if ((_h * (FC // 2) + _f // 2) % 2 == 0))
        _ns = 2 * (FC // 2) - _ng
        c.add("ldg", _ng * 16)
        c.add("lds", _ns * 16)

        @blk.tensor
        def _(t):
            for half in range(2):
                for fc in range(FC):
                    i = half * FC + fc
                    ld = half * (FC // 2) + fc // 2
                    if ld % 2 == 0:
                        t.wait_ge(s_ldg, bldg + (ld // 2 + 1) * 16)
                    else:
                        t.wait_ge(s_lds, blds_w + (ld // 2 + 1) * 16)
                    if fc == 0 and half == 1:
                        t.wait_ge(s_ve, bve + DH2)
                    for dc8 in range(DH2):
                        last = t.matmul(p_v[dc8][:], w2sl[:, ld % 4, fc % 2, dc8 * 128:(dc8 + 1) * 128],
                                        h_sb[:, fc, :], start=(fc == 0), stop=(fc == FC - 1))
                    last.then_inc(s_mm, 1)
        c.add("mm", 2 * FC)

        @blk.scalar
        def _(sc):
            for half in range(2):
                sc.wait_ge(s_mm, bmm + (half + 1) * FC)
                for dc8 in range(DH2):
                    dc = half * DH2 + dc8
                    sc.wait_ge(s_ve, bve + half * DH2 + dc8)
                    sc.activation(vf[:], p_v[dc8][:], AF.Identity,
                                  bias=b2cols[:, dc:dc + 1]).then_inc(s_ac, 1)
        c.add("ac", DC)

        @blk.vector
        def _(v):
            v = VG(v)
            v.wait_ge(s_ldg, bldg)
            for half in range(2):
                for dc8 in range(DH2):
                    i = half * DH2 + dc8
                    v.wait_ge(s_ac, bac + i + 1)
                    v.tensor_tensor(vb[:, i, :], vf[:], wg_bc[:], ALU.mult).then_inc(s_ve, 1)
        c.add("ve", DC)

        @blk.sync
        def _(sy):
            for half in range(2):
                for dc8 in range(DH2):
                    i = half * DH2 + dc8
                    dc = half * DH2 + dc8
                    sy.wait_ge(s_ve, bve + i + 1)
                    sy.dma_start(out=vals_s[dc * 128:(dc + 1) * 128, :], in_=vb[:, i, :]).then_inc(s_lds, 16)
        c.add("lds", DC * 16)

    if CUT == "b5c":
        return _finish(nc, c, ctx, [ctx_ffn2, ctx_ffn], s_lds, s_ldg, s_out, _ncd)

    ctx_ffn2.close()

    vals_sm = fsb("vals_sm", [128, SC, D], BF16)
    with nc.Block() as blk:
        blds = c.get("lds")

        @blk.sync
        def _(sy):
            sy.wait_ge(s_lds, blds)
            for sc_i in range(SC):
                sy.dma_start_transpose(out=vals_sm[:, sc_i, :],
                                       in_=vals_s[:, sc_i * 128:(sc_i + 1) * 128]).then_inc(s_lds, 16)
            sy.wait_ge(s_lds, blds + SC * 16)
        c.add("lds", SC * 16)

    if CUT == "b5d":
        return _finish(nc, c, ctx, [ctx_ffn], s_lds, s_ldg, s_out, _ncd)

    with nc.Block() as blk, ExitStack() as lc:
        lsb = lambda n, shp, dt=F32: lc.enter_context(nc.sbuf_tensor(n, shp, dt))
        x2t = lsb("x2t_e", [128, 3, D], F32)
        fproc = lsb("fproc", [128, 2, D], F32)
        y32 = lsb("y32_e", [128, 2, D], F32)
        ff = lsb("ff", [128, 3, D], F32)
        outt = lsb("outt", [128, D], F32)
        acc4e2 = lsb("acc4e2", [128, 2, 2], F32)
        statbe = lsb("statbe", [128, 2, 2], F32)
        bn2 = lsb("bn2_e", [128, 3, 2], F32)
        rc1 = lsb("rc1_e", [128, 1], F32)
        mr1 = lsb("mr1_e", [128, 1], F32)
        p_f = [lc.enter_context(nc.psum_tensor("p_f%d" % k, [128, 512], F32))
               for k in range(8)]
        bmm, bve, bac, blds = c.get("mm"), c.get("ve"), c.get("ac"), c.get("lds")
        bout = c.get("out")
        NAe = DSP + 3
        NVe = 3

        def posHe(tc):
            return 0 if tc == 0 else (DSP + 1) + NAe * (tc - 1)

        def posTe(tc):
            return 2 * (DSP + 1) + NAe * tc if tc <= TT - 2 else NAe * TT - 2

        dve_ops = [("A", 0), ("A", 1)]
        for _tc in range(TT):
            dve_ops.append(("S", _tc))
            if _tc + 2 < TT:
                dve_ops.append(("A", _tc + 2))
            dve_ops.append(("C", _tc))
        evA, evS, evC = {}, {}, {}
        for _i, (_kind, _tc) in enumerate(dve_ops):
            {"A": evA, "S": evS, "C": evC}[_kind][_tc] = bve + _i + 1

        bldg5 = c.get("ldg")

        @blk.gpsimd
        def _(g):
            for tc in range(TT):
                if tc >= 3:
                    g.wait_ge(s_ve, evC[tc - 3])
                g.dma_start(out=x2t[:, tc % 3, :],
                            in_=x2_s[tc * 128:(tc + 1) * 128, :]).then_inc(s_ldg, 16)
        c.add("ldg", TT * 16)

        @blk.sync
        def _(sy):
            for tc in range(TT):
                if DEBUG_FPROC:
                    sy.wait_ge(s_ve, evA[tc])
                    sy.dma_start(out=fdbg[tc], in_=fproc[:, tc % 2, :]).then_inc(s_out, 16)
                sy.wait_ge(s_ve, evC[tc])
                sy.dma_start(out=out[tc * 128:(tc + 1) * 128, :], in_=outt[:]).then_inc(s_out, 16)
            sy.wait_ge(s_out, bout + TT * 16 * (2 if DEBUG_FPROC else 1))
        c.add("out", TT * 16 * (2 if DEBUG_FPROC else 1))

        @blk.tensor
        def _(t):
            for tc in range(TT):
                for dsp in range(DSP):
                    i = tc * DSP + dsp
                    if i >= 8:
                        j = i - 8
                        t.wait_ge(s_ac, bac + posHe(j // DSP) + (j % DSP) + 1)
                    for sc_i in range(SC):
                        last = t.matmul(p_f[i % 8][:], PT_sb[:, sc_i, tc * 128:(tc + 1) * 128],
                                        vals_sm[:, sc_i, dsp * 512:(dsp + 1) * 512],
                                        start=(sc_i == 0), stop=(sc_i == SC - 1))
                    last.then_inc(s_mm, 1)
        c.add("mm", TT * DSP)

        @blk.scalar
        def _(sc):

            def emit_H(tc):
                if tc >= 2:
                    sc.wait_ge(s_ve, evA[tc - 2])
                for dsp in range(DSP):
                    sc.wait_ge(s_mm, bmm + tc * DSP + dsp + 1)
                    sc.activation(fproc[:, tc % 2, dsp * 512:(dsp + 1) * 512],
                                  p_f[(tc * DSP + dsp) % 8][:], AF.Identity).then_inc(s_ac, 1)
                sc.wait_ge(s_ldg, bldg5 + (tc + 1) * 16)
                sc.activation(y32[:, tc % 2, :], x2t[:, tc % 3, :], AF.Identity,
                              bias=m2r2[:, tc:tc + 1], scale=r2col[:, tc:tc + 1]).then_inc(s_ac, 1)

            def emit_T(tc):
                sc.wait_ge(s_ve, evA[tc])
                sc.activation(fproc[:, tc % 2, :], ff[:, tc % 3, :], AF.Square,
                              accum_out=acc4e2[:, tc % 2, 0:1]).then_inc(s_ac, 1)
                sc.wait_ge(s_ve, evS[tc])
                sc.activation(statbe[:, tc % 2, 1:2], statbe[:, tc % 2, 0:1], AF.Sqrt).then_inc(s_ac, 1)

            for tc in range(TT):
                emit_H(tc)
                if tc >= 1:
                    emit_T(tc - 1)
            emit_T(TT - 1)
        c.add("ac", TT * NAe)

        @blk.vector
        def _(v):
            v = VG(v)
            def emit_A(tc):
                v.wait_ge(s_ac, bac + posHe(tc) + DSP + 1)
                v.scalar_tensor_tensor(ff[:, tc % 3, :], y32[:, tc % 2, :], nselc[:, tc:tc + 1],
                                       fproc[:, tc % 2, :], ALU.mult, ALU.add,
                                       accum_out=acc4e2[:, tc % 2, 1:2]).then_inc(s_ve, 1)

            def emit_S(tc):
                v.wait_ge(s_ac, bac + posTe(tc) + 1)
                v.tensor_scalar(bn2[:, tc % 3, 0:1], acc4e2[:, tc % 2, 1:2], 1.0 / D, None, ALU.mult)
                v.tensor_scalar(bn2[:, tc % 3, 1:2], acc4e2[:, tc % 2, 0:1], 1.0 / D, None, ALU.mult)
                v.tensor_mul(small[:, 3:4], bn2[:, tc % 3, 0:1], bn2[:, tc % 3, 0:1])
                v.tensor_tensor(bn2[:, tc % 3, 1:2], bn2[:, tc % 3, 1:2], small[:, 3:4], ALU.subtract)
                v.tensor_scalar(statbe[:, tc % 2, 0:1], bn2[:, tc % 3, 1:2], float(LN_EPS), None, ALU.add).then_inc(s_ve, 1)

            def emit_C(tc):
                v.wait_ge(s_ac, bac + posTe(tc) + 2)
                v_rsqrt(v, statbe[:, tc % 2, 0:1], statbe[:, tc % 2, 1:2], rc1[:], small[:, 2:3])
                v.tensor_mul(mr1[:], bn2[:, tc % 3, 0:1], rc1[:])
                v.tensor_scalar(mr1[:], mr1[:], -1.0, None, ALU.mult)
                if tc >= 1:
                    v.wait_ge(s_out, bout + 16 * tc * (2 if DEBUG_FPROC else 1))
                v.scalar_tensor_tensor(outt[:], ff[:, tc % 3, :], rc1[:], x2t[:, tc % 3, :],
                                       ALU.mult, ALU.add)
                v.tensor_scalar(outt[:], outt[:], mr1[:], None, ALU.add).then_inc(s_ve, 1)

            for kind, tc in dve_ops:
                {"A": emit_A, "S": emit_S, "C": emit_C}[kind](tc)
        c.add("ve", TT * NVe)

    ctx_ffn.close()

    with nc.Block() as blk:
        @blk.sync
        def _(sy):
            sy.wait_ge(s_lds, c.get("lds"))
            sy.wait_ge(s_out, c.get("out"))

        @blk.gpsimd
        def _(g):
            g.wait_ge(s_ldg, c.get("ldg"))

    ctx.close()
    _ncd.__exit__(None, None, None)
    _audit(nc, c, {"s_lds": "lds", "s_ldg": "ldg", "s_mm": "mm", "s_ve": "ve",
                   "s_ac": "ac", "s_gp": "gp", "s_out": "out"})
    return nc


def shard_inputs(C, inputs, consts):
    x = np.ascontiguousarray(inputs["L0_attn_x"], dtype=np.float32)
    B = x.shape[0]
    base = {
        "W_cq": inputs["L0_attn_W_cq"], "q_scale": inputs["L0_attn_q_scale"],
        "W_dq_nope": inputs["L0_attn_W_dq_nope"], "W_dq_rope": inputs["L0_attn_W_dq_rope"],
        "W_ckv": inputs["L0_attn_W_ckv"], "kv_scale": inputs["L0_attn_kv_scale"],
        "W_dk_nope": inputs["L0_attn_W_dk_nope"], "W_dv": inputs["L0_attn_W_dv"],
        "W_krope": inputs["L0_attn_W_krope"], "W_o": inputs["L0_attn_W_o"],
        "W_r": inputs["L0_ffn_W_r"], "b_r": inputs["L0_ffn_b_r"],
        "W1": inputs["L0_ffn_W1"], "b1": inputs["L0_ffn_b1"],
        "W2": inputs["L0_ffn_W2"], "b2": inputs["L0_ffn_b2"],
    }
    base = {k: np.ascontiguousarray(v, dtype=np.float32) for k, v in base.items()}
    base.update(consts)
    return [dict(base, x=np.ascontiguousarray(x[i])) for i in range(B)]


def kernel(**inputs):
    C = FULL_CFG
    consts = host_consts(C)
    nc = build_nc(C)
    in_maps = shard_inputs(C, inputs, consts)
    res = run_bass_kernel_spmd(nc, in_maps, core_ids=list(range(len(in_maps))))
    return np.stack([r["out"] for r in res.results], axis=0)


# version 27
# speedup vs baseline: 1.0043x; 1.0043x over previous
import math
import numpy as np
from contextlib import ExitStack

import concourse.bass as bass
import concourse.mybir as mybir
from concourse.bass_utils import run_bass_kernel_spmd

F32 = mybir.dt.float32
BF16 = mybir.dt.bfloat16
I32 = mybir.dt.int32
AF = mybir.ActivationFunctionType
ALU = mybir.AluOpType
AX = mybir.AxisListType

DEBUG_FPROC = False
CUT = None
CUT_HEADS = None
DBG_SCRATCH = False
FULL_CFG = dict(S=2048, D=2048, QR=1024, KVR=512, NH=16, DFF=8192, KSEL=512)
NOPE, ROPE, VD = 128, 64, 128
DH = NOPE + ROPE
LN_EPS = 1e-5
RMS_EPS = 1e-6
ROPE_BASE = 10000.0
TWO_PI = 2.0 * math.pi


def host_consts(C):
    KSEL = C["KSEL"]
    SC = KSEL // 128
    ident32 = np.eye(128, dtype=np.float32)
    dmasks = np.zeros((4, 128, 512), dtype=np.float32)
    kk = np.arange(128)[:, None]
    qq = np.arange(512)[None, :]
    for d in range(4):
        dmasks[d] = (qq >= 128 * d + kk).astype(np.float32)
    iotak = np.broadcast_to(np.arange(KSEL, dtype=np.float32), (128, KSEL)).copy()
    iotacols = (np.arange(128, dtype=np.float32)[:, None]
                + 128.0 * np.arange(SC, dtype=np.float32)[None, :]).copy()
    S = C["S"]
    NH = C["NH"]
    half = 32
    inv = (1.0 / 10000.0 ** (np.arange(half, dtype=np.float32) * 2.0
                             / 64.0)).astype(np.float32)
    pos = np.arange(S, dtype=np.float32)
    ang = (inv[:, None] * pos[None, :]).astype(np.float32)
    cosq = np.cos(ang).astype(np.float32)
    sinq = np.sin(ang).astype(np.float32)
    tabcs = np.empty((128, 4, S), dtype=np.float32)
    tabcs[:, 0, :] = np.tile(cosq, (4, 1))
    tabcs[:, 1, :] = np.tile(sinq, (4, 1))
    tabcs[:, 2, :] = np.tile(cosq / NH, (4, 1))
    tabcs[:, 3, :] = np.tile(sinq / NH, (4, 1))
    return dict(ident32=ident32, dmasks=dmasks, iotak=iotak, iotacols=iotacols,
                tabcs=tabcs)



class VG:
    _OPS = {
        "tensor_scalar": (0, (1, 2), ("accum_out",)),
        "tensor_single_scalar": (0, (1, 2), ()),
        "tensor_tensor": (0, (1, 2), ()),
        "tensor_add": (0, (1, 2), ()),
        "tensor_sub": (0, (1, 2), ()),
        "tensor_mul": (0, (1, 2), ()),
        "tensor_max": (0, (1, 2), ()),
        "tensor_copy": ("out", ("in_",), ()),
        "reduce_max": ("out", ("in_",), ()),
        "reduce_sum": ("out", ("in_",), ()),
        "tensor_reduce": (0, (1,), ()),
        "scalar_tensor_tensor": (0, (1, 2, 3), ("accum_out",)),
        "reciprocal": (0, (1,), ()),
        "tensor_tensor_scan": (0, (1, 2, 3), ()),
        "memset": (0, (), ()),
        "select": (0, (1, 2, 3), ()),
        "bn_stats": ("out", ("in_",), ()),
        "bn_aggr": ("out", ("in_",), ()),
    }
    WINDOW = 8

    def __init__(self, v):
        self._v = v
        self._hist = []

    def __getattr__(self, name):
        f = getattr(self._v, name)
        spec = self._OPS.get(name)
        if spec is None:
            return f
        wpos, rpos, wkw = spec

        def _name_of(x):
            try:
                return x.tensor.name
            except Exception:
                return None

        def wrapped(*args, **kwargs):
            def get(p):
                if isinstance(p, int):
                    return args[p] if p < len(args) else None
                return kwargs.get(p)
            writes = set()
            reads = set()
            w = get(wpos) if not isinstance(wpos, str) else kwargs.get(wpos, args[0] if args else None)
            n = _name_of(w)
            if n:
                writes.add(n)
            for p in rpos:
                n = _name_of(get(p))
                if n:
                    reads.add(n)
            for p in wkw:
                n = _name_of(kwargs.get(p))
                if n:
                    writes.add(n)
            conflict = False
            for hw, hr in self._hist:
                if (reads & hw) or (writes & hw) or (writes & hr):
                    conflict = True
                    break
            if conflict:
                self._v.drain()
                self._hist.clear()
            ins = f(*args, **kwargs)
            self._hist.append((writes, reads))
            if len(self._hist) > self.WINDOW:
                self._hist.pop(0)
            return ins
        return wrapped


class Ctr:
    def __init__(self):
        self.v = {}
    def add(self, k, n=1):
        self.v[k] = self.v.get(k, 0) + n
        return self.v[k]
    def get(self, k):
        return self.v.get(k, 0)



def _audit(nc, c, mapping):
    import collections
    got = collections.Counter()
    for ins in nc.all_instructions():
        si = getattr(ins, "sync_info", None)
        ups = getattr(si, "on_update", None) if si is not None else None
        if not ups:
            continue
        for u in ups:
            nm = str(getattr(u, "ant_name", ""))
            if nm in mapping:
                got[nm] += int(getattr(u, "update_value", 1) or 1)
    bad = []
    for key, ck in mapping.items():
        exp = c.get(ck)
        if got.get(key, 0) != exp:
            bad.append(f"{key}: emitted {got.get(key,0)} expected {exp}")
    if bad:
        raise AssertionError("sem audit: " + "; ".join(bad))


def _finish(nc, c, ctx, stacks, s_lds, s_ldg, s_out, ncd):
    with nc.Block() as blk:
        @blk.sync
        def _(sy):
            sy.wait_ge(s_lds, c.get("lds"))
            if c.get("out"):
                sy.wait_ge(s_out, c.get("out"))

        @blk.gpsimd
        def _(g):
            g.wait_ge(s_ldg, c.get("ldg"))
    for st_ in stacks:
        st_.close()
    ctx.close()
    ncd.__exit__(None, None, None)
    _audit(nc, c, {"s_lds": "lds", "s_ldg": "ldg", "s_mm": "mm", "s_ve": "ve",
                   "s_ac": "ac", "s_gp": "gp", "s_out": "out"})
    return nc


def build_nc(C):
    S, D, QR, KVR, NH, DFF, KSEL = (C["S"], C["D"], C["QR"], C["KVR"],
                                    C["NH"], C["DFF"], C["KSEL"])
    TT, DC, QC, KC, FC = S // 128, D // 128, QR // 128, KVR // 128, DFF // 128
    SP = S // 512
    SC = KSEL // 128
    HG = NH // 4
    DSP = D // 512

    nc = bass.Bass("TRN2", target_bir_lowering=False, debug=False,
                   detect_race_conditions=False)
    _ncd = nc.allow_non_contiguous_dma(reason="small strided const/stat loads")
    _ncd.__enter__()

    din = lambda n, shp: nc.dram_tensor(n, shp, F32, kind="ExternalInput").ap()
    x_in = din("x", [S, D])
    W_cq, q_scale = din("W_cq", [D, QR]), din("q_scale", [QR])
    W_dq_nope, W_dq_rope = din("W_dq_nope", [QR, NH * NOPE]), din("W_dq_rope", [QR, NH * ROPE])
    W_ckv, kv_scale = din("W_ckv", [D, KVR]), din("kv_scale", [KVR])
    W_dk_nope, W_dv = din("W_dk_nope", [KVR, NH * NOPE]), din("W_dv", [KVR, NH * VD])
    W_krope, W_o = din("W_krope", [D, ROPE]), din("W_o", [NH * VD, D])
    W_r, b_r = din("W_r", [D, 1]), din("b_r", [1])
    W1, b1 = din("W1", [D, DFF]), din("b1", [DFF])
    W2, b2 = din("W2", [DFF, D]), din("b2", [D])
    ident32 = din("ident32", [128, 128])
    dmasks = din("dmasks", [4, 128, 512])
    tabcs = din("tabcs", [128, 4, C["S"]])
    iotak = din("iotak", [128, KSEL])
    iotacols = din("iotacols", [128, SC])

    out = nc.dram_tensor("out", [S, D], F32, kind="ExternalOutput").ap()

    scr = lambda n, shp, dt: nc.dram_tensor(
        n, shp, dt, kind=("ExternalOutput" if DBG_SCRATCH else "Internal")).ap()
    x1_s = scr("x1_s", [S, D], BF16)
    x2_s = scr("x2_s", [S, D], F32)
    o_s = scr("o_s", [TT, NH, VD, 128], BF16)
    y_s = scr("y_s", [S, D], BF16)
    vals_s = scr("vals_s", [D, KSEL], BF16)
    rt_s = scr("rt_s", [1, S], F32)
    grow_s = scr("grow_s", [2, S], BF16)
    rl_s = scr("rl_s", [2, 512], F32)
    wrow_s = scr("wrow_s", [1, S], F32)
    selrow_s = scr("selrow_s", [1, S], F32)
    wselrow_s = scr("wselrow_s", [1, KSEL], F32)
    w1b_s = scr("w1b_s", [D, DFF], BF16)
    w2b_s = scr("w2b_s", [DFF, D], BF16)
    wob_s = scr("wob_s", [NH * VD, D], BF16)
    adbg = scr("adbg", [128, D], F32) if DBG_SCRATCH else None
    otdbg = scr("otdbg", [128, 16], F32) if DBG_SCRATCH else None
    fdbg = scr("fdbg", [S // 128, 128, D], F32) if DEBUG_FPROC else None

    ctx = ExitStack()
    sb = lambda n, shp, dt=F32: ctx.enter_context(nc.sbuf_tensor(n, shp, dt))
    sem = lambda n: ctx.enter_context(nc.semaphore(n))

    idbf = sb("idbf", [128, 128], BF16)
    id32 = sb("id32", [128, 128], F32)
    iotak_sb = sb("iotak_sb", [128, KSEL], F32)
    iotacols_sb = sb("iotacols_sb", [128, SC], F32)
    ones_col = sb("ones_col", [128, 1], BF16)
    m2r2 = sb("m2r2", [128, TT], F32)
    r2col = sb("r2col", [128, TT], F32)
    selc = sb("selc", [128, TT], F32)
    nselc = sb("nselc", [128, TT], F32)
    w16 = sb("w16", [128, TT], F32)
    small = sb("small", [128, 16], F32)
    rowA = sb("rowA", [1, S], F32)
    rowB = sb("rowB", [1, S], F32)

    s_ldg = sem("s_ldg")
    s_lds = sem("s_lds")
    s_mm = sem("s_mm")
    s_ve = sem("s_ve")
    s_ac = sem("s_ac")
    s_gp = sem("s_gp")
    s_out = sem("s_out")
    c = Ctr()

    def v_stats(v, x_ap, junk_ap, bn2_ap, tmp1_ap, width):
        v.tensor_scalar(junk_ap, x_ap, 1.0, None, ALU.mult, ALU.add,
                        accum_out=bn2_ap[:, 0:1])
        v.scalar_tensor_tensor(junk_ap, x_ap, 1.0, x_ap, ALU.mult, ALU.mult,
                               accum_out=bn2_ap[:, 1:2])
        v.tensor_scalar(bn2_ap[:, 0:1], bn2_ap[:, 0:1], 1.0 / width, None, ALU.mult)
        v.tensor_scalar(bn2_ap[:, 1:2], bn2_ap[:, 1:2], 1.0 / width, None, ALU.mult)
        v.tensor_mul(tmp1_ap, bn2_ap[:, 0:1], bn2_ap[:, 0:1])
        v.tensor_tensor(bn2_ap[:, 1:2], bn2_ap[:, 1:2], tmp1_ap, ALU.subtract)

    def v_rsqrt(v, x_ap, s_ap, r_ap, t_ap):
        v.reciprocal(r_ap, s_ap)
        v.tensor_mul(t_ap, r_ap, r_ap)
        v.tensor_mul(t_ap, t_ap, x_ap)
        v.tensor_scalar(t_ap, t_ap, -0.5, 1.5, ALU.mult, ALU.add)
        v.tensor_mul(r_ap, r_ap, t_ap)

    ctx_att = ExitStack()
    asb = lambda n, shp, dt=F32: ctx_att.enter_context(nc.sbuf_tensor(n, shp, dt))
    cq_bf = asb("cq_bf", [128, QC, S], BF16)
    ckv_bf = asb("ckv_bf", [128, KC, S], BF16)
    kx_rope = asb("kx_rope", [65, S], BF16)
    kev_raw = asb("kev_raw", [32, S], BF16)
    kod_raw = asb("kod_raw", [32, S], BF16)
    tabs = asb("tabs", [128, 4, S], BF16)
    dmask_sb = asb("dmask_sb", [128, 4, 512], BF16)
    qscol = asb("qscol", [128, QC], F32)
    kvscol = asb("kvscol", [128, KC], F32)

    with nc.Block() as blk:
        blds = c.get("lds")

        @blk.gpsimd
        def _(g):
            g.dma_start(out=dmask_sb[:], in_=dmasks.rearrange("f p m -> p f m")).then_inc(s_ldg, 16)
            g.dma_start(out=tabs[:], in_=tabcs).then_inc(s_ldg, 16)
        c.add("ldg", 32)

        @blk.sync
        def _(sy):
            for ap_dst, ap_src in [
                (id32[:], ident32[:]),
                (iotak_sb[:], iotak[:]),
                (iotacols_sb[:], iotacols[:]),
                (qscol[:], q_scale.rearrange("(c p) -> p c", p=128)),
                (kvscol[:], kv_scale.rearrange("(c p) -> p c", p=128)),
                (rowA[0:1, 0:D], W_r.rearrange("d u -> u d")),
                (rowB[0:1, 0:1], b_r.unsqueeze(0)),
            ]:
                sy.dma_start(out=ap_dst, in_=ap_src).then_inc(s_lds, 16)
        c.add("lds", 7 * 16)

        @blk.vector
        def _(v):
            v = VG(v)
            v.memset(ones_col[:], 1.0)
            v.memset(kx_rope[64:65, :], 1.0)
            v.wait_ge(s_lds, blds + 7 * 16)
            v.tensor_copy(out=idbf[:], in_=id32[:])
        c.add("ve", 0)

    ctx_pre = ExitStack()
    psb = lambda n, shp, dt=F32: ctx_pre.enter_context(nc.sbuf_tensor(n, shp, dt))
    w_cq_bf = psb("w_cq_bf", [128, DC, QR], BF16)
    w_ckv_bf = psb("w_ckv_bf", [128, DC, KVR], BF16)
    w_kro_bf = psb("w_kro_bf", [128, DC, ROPE], BF16)
    x1f = psb("x1f", [128, 2, DC, 512], BF16)

    if CUT == "b0a":
        return _finish(nc, c, ctx, [ctx_att], s_lds, s_ldg, s_out, _ncd)

    with nc.Block() as blk, ExitStack() as lc:
        lsb = lambda n, shp, dt=F32: lc.enter_context(nc.sbuf_tensor(n, shp, dt))
        x0t = lsb("x0t", [128, 2, D], F32)
        x1t = lsb("x1t", [128, 2, D], BF16)
        bn6 = lsb("bn6", [128, 6 * (D // 512)], F32)
        bn2 = lsb("bn2", [128, 2], F32)
        rcol = lsb("rcol", [128, 1], F32)
        mrneg = lsb("mrneg", [128, 1], F32)
        bve, bac, blds = c.get("ve"), c.get("ac"), c.get("lds")

        @blk.gpsimd
        def _(g):
            g.dma_start(out=w_cq_bf[:], in_=W_cq.rearrange("(c p) m -> p c m", p=128)).then_inc(s_ldg, 16)
            g.dma_start(out=w_ckv_bf[:], in_=W_ckv.rearrange("(c p) m -> p c m", p=128)).then_inc(s_ldg, 16)
            g.dma_start(out=w_kro_bf[:], in_=W_krope.rearrange("(c p) m -> p c m", p=128)).then_inc(s_ldg, 16)
        c.add("ldg", 3 * 16)

        @blk.sync
        def _(sy):
            for tt in range(TT):
                if tt >= 2:
                    sy.wait_ge(s_ac, bac + 2 * (tt - 2) + 2)
                sy.dma_start(out=x0t[:, tt % 2, :], in_=x_in[tt * 128:(tt + 1) * 128, :]).then_inc(s_lds, 16)
                sy.wait_ge(s_ac, bac + 2 * tt + 2)
                sy.dma_start(out=x1_s[tt * 128:(tt + 1) * 128, :], in_=x1t[:, tt % 2, :]).then_inc(s_lds, 16)
        c.add("lds", TT * 32)

        @blk.vector
        def _(v):
            v = VG(v)
            for tt in range(TT):
                v.wait_ge(s_lds, blds + tt * 32 + 16)
                xt = x0t[:, tt % 2, :]
                if tt >= 2:
                    v.wait_ge(s_lds, blds + 32 * (tt - 2) + 32)
                v_stats(v, xt, x1t[:, tt % 2, :], bn2[:], small[:, 3:4], D)
                v.tensor_scalar(small[:, 0:1], bn2[:, 1:2], float(LN_EPS), None, ALU.add).then_inc(s_ve, 1)
                v.wait_ge(s_ac, bac + 2 * tt + 1)
                v_rsqrt(v, small[:, 0:1], small[:, 1:2], rcol[:], small[:, 2:3])
                v.tensor_mul(mrneg[:], bn2[:, 0:1], rcol[:])
                v.tensor_scalar(mrneg[:], mrneg[:], -1.0, None, ALU.mult).then_inc(s_ve, 1)
        c.add("ve", 2 * TT)

        @blk.scalar
        def _(sc):
            for tt in range(TT):
                sc.wait_ge(s_ve, bve + 2 * tt + 1)
                sc.activation(small[:, 1:2], small[:, 0:1], AF.Sqrt).then_inc(s_ac, 1)
                sc.wait_ge(s_ve, bve + 2 * tt + 2)
                if tt >= 2:
                    sc.wait_ge(s_lds, blds + 32 * (tt - 2) + 32)
                sc.activation(x1t[:, tt % 2, :], x0t[:, tt % 2, :], AF.Identity,
                              bias=mrneg[:], scale=rcol[:]).then_inc(s_ac, 1)
        c.add("ac", 2 * TT)

    if CUT == "b0b":
        return _finish(nc, c, ctx, [ctx_pre, ctx_att], s_lds, s_ldg, s_out, _ncd)

    with nc.Block() as blk, ExitStack() as lc:
        p_rot = [lc.enter_context(nc.psum_tensor("p_rot%d" % k, [128, 512], F32))
                 for k in range(4)]
        bmm, bac, bldg, blds = c.get("mm"), c.get("ac"), c.get("ldg"), c.get("lds")
        sjobs = ([("cq", qrc) for qrc in range(QC)]
                 + [("ckv", kc) for kc in range(KC)]
                 + [("kev", 0), ("kod", 0)])
        NJ = len(sjobs)

        @blk.sync
        def _(sy):
            sy.wait_ge(s_lds, blds)
            for sp in range(SP):
                if sp >= 2:
                    sy.wait_ge(s_mm, bmm + (sp - 1) * NJ)
                for dc in range(DC):
                    sy.dma_start_transpose(
                        out=x1f[:, sp % 2, dc, :],
                        in_=x1_s[sp * 512:(sp + 1) * 512,
                                 dc * 128:(dc + 1) * 128]).then_inc(s_lds, 16)
        c.add("lds", SP * DC * 16)

        @blk.tensor
        def _(t):
            t.wait_ge(s_ldg, bldg)
            for sp in range(SP):
                t.wait_ge(s_lds, blds + (sp + 1) * DC * 16)
                for j, (kind, cc) in enumerate(sjobs):
                    i = sp * NJ + j
                    if i >= 4:
                        t.wait_ge(s_ac, bac + i - 3)
                    for dc in range(DC):
                        rhs = x1f[:, sp % 2, dc, :]
                        if kind == "cq":
                            last = t.matmul(p_rot[i % 4][:], w_cq_bf[:, dc, cc * 128:(cc + 1) * 128],
                                            rhs, start=(dc == 0), stop=(dc == DC - 1))
                        elif kind == "ckv":
                            last = t.matmul(p_rot[i % 4][:], w_ckv_bf[:, dc, cc * 128:(cc + 1) * 128],
                                            rhs, start=(dc == 0), stop=(dc == DC - 1))
                        else:
                            wv = w_kro_bf[:, dc, :].rearrange("p (j two) -> p j two", two=2)
                            lhsT = wv[:, :, 0] if kind == "kev" else wv[:, :, 1]
                            last = t.matmul(p_rot[i % 4][0:32, :], lhsT, rhs,
                                            start=(dc == 0), stop=(dc == DC - 1))
                    last.then_inc(s_mm, 1)
        c.add("mm", SP * NJ)

        @blk.scalar
        def _(sc):
            for sp in range(SP):
                sl = slice(sp * 512, (sp + 1) * 512)
                for j, (kind, cc) in enumerate(sjobs):
                    i = sp * NJ + j
                    sc.wait_ge(s_mm, bmm + i + 1)
                    if kind == "cq":
                        dst, srcp = cq_bf[:, cc, sl], p_rot[i % 4][:]
                    elif kind == "ckv":
                        dst, srcp = ckv_bf[:, cc, sl], p_rot[i % 4][:]
                    elif kind == "kev":
                        dst, srcp = kev_raw[:, sl], p_rot[i % 4][0:32, :]
                    else:
                        dst, srcp = kod_raw[:, sl], p_rot[i % 4][0:32, :]
                    sc.activation(dst, srcp, AF.Identity).then_inc(s_ac, 1)
        c.add("ac", SP * NJ)

    if CUT == "b1":
        return _finish(nc, c, ctx, [ctx_pre, ctx_att], s_lds, s_ldg, s_out, _ncd)

    with nc.Block() as blk, ExitStack() as lc:
        lsb = lambda n, shp, dt=F32: lc.enter_context(nc.sbuf_tensor(n, shp, dt))
        p_g = [lc.enter_context(nc.psum_tensor("p_g%d" % k, [128, 128], F32))
               for k in range(2)]
        p_row = lc.enter_context(nc.psum_tensor("p_row", [1, S], F32))
        ssq = lsb("ssq", [128, TT], F32)
        sskv = lsb("sskv", [128, TT], F32)
        junk = lsb("junk", [128, 128], F32)
        gt = lsb("gt", [128, TT], F32)
        gs = lsb("gs", [128, TT], F32)
        gr = lsb("gr", [128, TT], F32)
        gsc = lsb("gsc", [128, TT], F32)
        g16b = lsb("g16b", [128, 2, TT], BF16)
        grow = lsb("grow", [1, 2, S], BF16)
        gbc = lsb("gbc", [128, 2, S], BF16)
        bmm, bve, bac, bgp = c.get("mm"), c.get("ve"), c.get("ac"), c.get("gp")
        blds_c, bldg_c = c.get("lds"), c.get("ldg")

        @blk.tensor
        def _(t):
            for i in range(2 * TT):
                kind, tt = ("q", i) if i < TT else ("kv", i - TT)
                if i >= 2:
                    t.wait_ge(s_ve, bve + i - 1)
                sl = slice(tt * 128, (tt + 1) * 128)
                nch, src = (QC, cq_bf) if kind == "q" else (KC, ckv_bf)
                for cc in range(nch):
                    last = t.matmul(p_g[i % 2][:], src[:, cc, sl], src[:, cc, sl],
                                    start=(cc == 0), stop=(cc == nch - 1))
                last.then_inc(s_mm, 1)
            for z in range(2):
                t.wait_ge(s_ve, bve + 2 * TT + (2 if z == 0 else 5))
                for tt in range(TT):
                    last = t.matmul(p_row[0:1, tt * 128:(tt + 1) * 128], g16b[:, z, tt:tt + 1],
                                    idbf[:], start=True, stop=True)
                last.then_inc(s_mm, 1)
        c.add("mm", 2 * TT + 2)

        @blk.scalar
        def _(sc):
            for z in range(2):
                sc.wait_ge(s_ve, bve + 2 * TT + (1 if z == 0 else 3))
                sc.activation(gs[:], gt[:], AF.Sqrt).then_inc(s_ac, 1)
        c.add("ac", 2)

        @blk.vector
        def _(v):
            v = VG(v)
            for i in range(2 * TT):
                kind, tt = ("q", i) if i < TT else ("kv", i - TT)
                v.wait_ge(s_mm, bmm + i + 1)
                dst = ssq if kind == "q" else sskv
                v.scalar_tensor_tensor(junk[:], p_g[i % 2][:], 1.0, id32[:],
                                       ALU.mult, ALU.mult, accum_out=dst[:, tt:tt + 1]).then_inc(s_ve, 1)
            for z, (ss, aa, bb) in enumerate([(ssq, float(DH) / QR, float(DH) * RMS_EPS),
                                              (sskv, 1.0 / KVR, RMS_EPS)]):
                v.tensor_scalar(gt[:], ss[:], aa, bb, ALU.mult, ALU.add).then_inc(s_ve, 1)
                v.wait_ge(s_ac, bac + z + 1)
                v_rsqrt(v, gt[:], gs[:], gr[:], gsc[:])
                v.tensor_copy(out=g16b[:, z, :], in_=gr[:]).then_inc(s_ve, 1)
            v.wait_ge(s_mm, bmm + 2 * TT + 1)
            v.tensor_copy(out=grow[0:1, 0, :], in_=p_row[0:1, :]).then_inc(s_ve, 1)
            v.wait_ge(s_mm, bmm + 2 * TT + 2)
            v.tensor_copy(out=grow[0:1, 1, :], in_=p_row[0:1, :]).then_inc(s_ve, 1)
            v.wait_ge(s_ldg, bldg_c + 32)
            for qrc in range(QC):
                v.tensor_tensor(cq_bf[:, qrc, :], cq_bf[:, qrc, :], gbc[:, 0, :], ALU.mult)
            for kc in range(KC):
                v.tensor_tensor(ckv_bf[:, kc, :], ckv_bf[:, kc, :], gbc[:, 1, :], ALU.mult)
        c.add("ve", 2 * TT + 6)

        @blk.sync
        def _(sy):
            for z in range(2):
                sy.wait_ge(s_ve, bve + 2 * TT + 5 + z)
                sy.dma_start(out=grow_s[z:z + 1, :], in_=grow[0:1, z, :]).then_inc(s_lds, 16)
        c.add("lds", 32)

        @blk.gpsimd
        def _(g):
            for z in range(2):
                g.wait_ge(s_lds, blds_c + 16 * (z + 1))
                g.dma_start(out=gbc[:, z, :],
                            in_=grow_s[z:z + 1, :].to_broadcast([128, S])).then_inc(s_ldg, 16)
        c.add("ldg", 32)

    if CUT == "b1c":
        return _finish(nc, c, ctx, [ctx_pre, ctx_att], s_lds, s_ldg, s_out, _ncd)

    ctx_pre.close()

    wdq = asb("wdq", [128, 2, QC, 128], BF16)
    wro = asb("wro", [128, 2, QC, 256], BF16)
    wdk = asb("wdk", [128, 2, KC, 128], BF16)
    wdv = asb("wdv", [128, 2, KC, 128], BF16)
    qn_h = asb("qn_h", [128, 2, S], BF16)
    kn_h = asb("kn_h", [128, 2, S], BF16)
    v_h = asb("v_h", [128, 2, S], BF16)
    q_ext = asb("q_ext", [65, 2, S], BF16)
    qev_raw = asb("qev_raw", [128, S], BF16)
    qod_raw = asb("qod_raw", [128, S], BF16)
    qev_rot = asb("qev_rot", [128, S], BF16)
    qod_rot = asb("qod_rot", [128, S], BF16)
    bsc1 = asb("bsc1", [128, S], BF16)
    bsc2 = asb("bsc2", [128, S], BF16)
    pt_sb = asb("pt_sb", [128, 4, 512], BF16)
    o_f32 = asb("o_f32", [128, 2, 512], F32)
    o_bf = asb("o_bf", [128, 2, 512], BF16)
    rl_row = asb("rl_row", [1, 2, 512], F32)
    rl_bc = asb("rl_bc", [128, 2, 512], F32)
    mneg16 = asb("mneg16", [128, 16], F32)
    mneg16b = asb("mneg16b", [128, 16], BF16)

    def head_weight_loads(g, hh):
        b = hh % 2
        g.dma_start(out=wdq[:, b, :, :],
                    in_=W_dq_nope[:, hh * 128:(hh + 1) * 128].rearrange("(c p) m -> p c m", p=128)).then_inc(s_ldg, 16)
        g.dma_start(out=wdk[:, b, :, :],
                    in_=W_dk_nope[:, hh * 128:(hh + 1) * 128].rearrange("(c p) m -> p c m", p=128)).then_inc(s_ldg, 16)
        g.dma_start(out=wdv[:, b, :, :],
                    in_=W_dv[:, hh * 128:(hh + 1) * 128].rearrange("(c p) m -> p c m", p=128)).then_inc(s_ldg, 16)
        n = 3 * 16
        if hh % 4 == 0:
            gg = hh // 4
            g.dma_start(out=wro[:, (gg % 2), :, :],
                        in_=W_dq_rope[:, gg * 256:(gg + 1) * 256].rearrange(
                            "(c p) m -> p c m", p=128)).then_inc(s_ldg, 16)
            n += 16
        return n

    def head_weight_scale(v, hh):
        b = hh % 2
        last = None
        for qrc in range(QC):
            last = v.tensor_scalar(wdq[:, b, qrc, :], wdq[:, b, qrc, :], qscol[:, qrc:qrc + 1], None, ALU.mult)
        for kc in range(KC):
            last = v.tensor_scalar(wdk[:, b, kc, :], wdk[:, b, kc, :], kvscol[:, kc:kc + 1], None, ALU.mult)
            last = v.tensor_scalar(wdv[:, b, kc, :], wdv[:, b, kc, :], kvscol[:, kc:kc + 1], None, ALU.mult)
        if hh % 4 == 0:
            gg = hh // 4
            for qrc in range(QC):
                last = v.tensor_scalar(wro[:, gg % 2, qrc, :], wro[:, gg % 2, qrc, :],
                                       qscol[:, qrc:qrc + 1], None, ALU.mult)
        return last

    spans_nkt = [4 * sp + 4 for sp in range(SP)]
    NKT = sum(spans_nkt)
    LOOK = 2

    def emission_idx(nkt):
        st_i, pv_i = {}, {}
        n = 0
        for kt in range(nkt):
            n += 1; st_i[kt] = n
            if kt >= LOOK:
                n += 1; pv_i[kt - LOOK] = n
        for kt in range(max(0, nkt - LOOK), nkt):
            n += 1; pv_i[kt] = n
        return st_i, pv_i

    p1_groups = []
    for qt in range(TT):
        for ksp in range((128 * (qt + 1) + 511) // 512):
            p1_groups.append((qt, ksp))
    NG1 = len(p1_groups)

    NHEADS = CUT_HEADS if CUT_HEADS is not None else NH
    for k in range(NHEADS + 1):
        h = k
        hp = k - 1
        has_p1 = k < NHEADS
        has_p2 = k >= 1
        b = h % 2
        bp = hp % 2
        j = h % 4
        jobs = []
        if has_p1:
            jobs = [("qn", sp) for sp in range(SP)]
            if j == 0:
                jobs += [("qev", sp) for sp in range(SP)]
                jobs += [("qod", sp) for sp in range(SP)]
            jobs += [("kn", sp) for sp in range(SP)]
            jobs += [("v", tt) for tt in range(TT)]
        nproj = len(jobs)
        NQK = nproj - TT if has_p1 else 0
        nw = 0
        if k == 0:
            nw += 48 + 16
        if has_p1 and h + 1 < NH:
            nw += 48 + (16 if (h + 1) % 4 == 0 else 0)

        with nc.Block() as blk, ExitStack() as lc:
            pq = [lc.enter_context(nc.psum_tensor("pq_%d_%d" % (k, i), [128, 512], F32))
                  for i in range(3)]
            if has_p2:
                ppt = [lc.enter_context(nc.psum_tensor("ppt_%d_%d" % (k, i), [128, 512], F32))
                       for i in range(3)]
                po = lc.enter_context(nc.psum_tensor("po_%d" % k, [128, 512], F32))
                pl = lc.enter_context(nc.psum_tensor("pl_%d" % k, [33, 512], F32))
            bmm, bve, bac, bldg, blds = (c.get("mm"), c.get("ve"), c.get("ac"),
                                         c.get("ldg"), c.get("lds"))
            te_b = [bmm]
            for sp in range(SP):
                te_b.append(te_b[-1] + 2 * spans_nkt[sp])
            mP = bmm + (2 * NKT if has_p2 else 0)
            mB = mP + nproj
            ac_b = [bac]
            for sp in range(SP):
                ac_b.append(ac_b[-1] + spans_nkt[sp] + 1)
            aP = bac + ((NKT + SP) if has_p2 else 0)
            evM, evR, evN = {}, {}, {}
            _vec = bve
            for sp in range(SP):
                evM[sp] = []
                for d in range(4):
                    _vec += 1
                    evM[sp].append(_vec)
                _vec += 1
                evR[sp] = _vec
                if sp >= 1:
                    _vec += 1
                    evN[sp - 1] = _vec
            _vec += 1
            evN[SP - 1] = _vec
            vP = bve + (6 * SP if has_p2 else 0)
            SCL0 = 2 if k == 0 else 0
            ASM = vP + SCL0 + 1
            MNB = ASM + NG1 + 1
            gW = bldg + (SP * 16 if has_p2 else 0)

            @blk.gpsimd
            def _(g, k=k, h=h, bldg=bldg, blds=blds, has_p1=has_p1, has_p2=has_p2):
                if has_p2:
                    for sp in range(SP):
                        g.wait_ge(s_lds, blds + 32 * sp + 16)
                        g.dma_start(out=rl_bc[:, sp % 2, :],
                                    in_=rl_s[sp % 2:sp % 2 + 1, :].to_broadcast([128, 512])).then_inc(s_ldg, 16)
                if k == 0:
                    head_weight_loads(g, 0)
                if has_p1 and h + 1 < NH:
                    head_weight_loads(g, h + 1)
                if has_p1:
                    rw1 = D // NH
                    g.dma_start(out=w1b_s[h * rw1:(h + 1) * rw1, :],
                                in_=W1[h * rw1:(h + 1) * rw1, :]).then_inc(s_ldg, 16)
                    rw2 = DFF // NH
                    g.dma_start(out=w2b_s[h * rw2:(h + 1) * rw2, :],
                                in_=W2[h * rw2:(h + 1) * rw2, :]).then_inc(s_ldg, 16)
                    rwo = (NH * VD) // NH
                    g.dma_start(out=wob_s[h * rwo:(h + 1) * rwo, :],
                                in_=W_o[h * rwo:(h + 1) * rwo, :]).then_inc(s_ldg, 16)
            c.add("ldg", (SP * 16 if has_p2 else 0) + nw + (48 if has_p1 else 0))

            if has_p2:
                @blk.sync
                def _(sy, hp=hp, bldg=bldg, blds=blds, evR=evR, evN=evN):
                    sy.wait_ge(s_ldg, bldg)
                    for sp in range(SP):
                        if sp > 1:
                            sy.wait_ge(s_ldg, bldg + 16 * (sp - 1))
                        sy.wait_ge(s_ve, evR[sp])
                        sy.dma_start(out=rl_s[sp % 2:sp % 2 + 1, :],
                                     in_=rl_row[0:1, sp % 2, :]).then_inc(s_lds, 16)
                        sy.wait_ge(s_ve, evN[sp])
                        srcap = o_bf[:, sp % 2, :].rearrange("p (tt ti) -> p tt ti", ti=128)
                        dstap = o_s[4 * sp:4 * sp + 4, hp, :, :].transpose([1, 0, 2])
                        sy.dma_start(out=dstap, in_=srcap).then_inc(s_lds, 16)
                c.add("lds", SP * 32)

            @blk.tensor
            def _(t, k=k, b=b, bp=bp, jobs=jobs, te_b=te_b, ac_b=ac_b, evM=evM, evR=evR,
                  mP=mP, mB=mB, aP=aP, vP=vP, ASM=ASM, MNB=MNB, SCL0=SCL0,
                  has_p1=has_p1, has_p2=has_p2, NQK=NQK):
                if has_p2:
                    def emit_st(sp, kt):
                        if kt >= 3:
                            t.wait_ge(s_ac, ac_b[sp] + (kt - 3) + 1)
                        t.matmul(ppt[kt % 3][:], kn_h[:, bp, kt * 128:(kt + 1) * 128],
                                 qn_h[:, bp, sp * 512:(sp + 1) * 512], start=True, stop=False)
                        t.matmul(ppt[kt % 3][:], kx_rope[:, kt * 128:(kt + 1) * 128],
                                 q_ext[:, bp, sp * 512:(sp + 1) * 512], start=False, stop=True).then_inc(s_mm, 1)

                    def emit_pv(sp, kt, nkt):
                        t.wait_ge(s_ac, ac_b[sp] + kt + 1)
                        if kt >= 4 * sp:
                            t.wait_ge(s_ve, evM[sp][kt - 4 * sp])
                        if kt == 0 and sp >= 1:
                            t.wait_ge(s_ac, ac_b[sp])
                        if kt == 0 and sp > 1:
                            t.wait_ge(s_ve, evR[sp - 2])
                        plr = 32 * (sp % 2)
                        t.matmul(po[:], v_h[:, bp, kt * 128:(kt + 1) * 128], pt_sb[:, kt % 4, :],
                                 start=(kt == 0), stop=(kt == nkt - 1))
                        t.matmul(pl[plr:plr + 1, :], ones_col[:], pt_sb[:, kt % 4, :],
                                 start=(kt == 0), stop=(kt == nkt - 1)).then_inc(s_mm, 1)

                    for sp in range(SP):
                        nkt = spans_nkt[sp]
                        for kt in range(nkt):
                            emit_st(sp, kt)
                            if kt >= LOOK:
                                emit_pv(sp, kt - LOOK, nkt)
                        for kt in range(max(0, nkt - LOOK), nkt):
                            emit_pv(sp, kt, nkt)
                if has_p1:
                    for i, (kind, idx) in enumerate(jobs):
                        if i >= 3:
                            t.wait_ge(s_ac, aP + (i - 3) + 1)
                        if k == 0 and i == 0:
                            t.wait_ge(s_ve, vP + 1)
                        if kind == "qn":
                            for qrc in range(QC):
                                last = t.matmul(pq[i % 3][:], wdq[:, b, qrc, :],
                                                cq_bf[:, qrc, idx * 512:(idx + 1) * 512],
                                                start=(qrc == 0), stop=(qrc == QC - 1))
                        elif kind in ("qev", "qod"):
                            two = 0 if kind == "qev" else 1
                            for qrc in range(QC):
                                wv = wro[:, (h // 4) % 2, qrc, :].rearrange(
                                    "p (hh jj two) -> p hh jj two", hh=4, two=2)[:, :, :, two]
                                last = t.matmul(pq[i % 3][:], wv,
                                                cq_bf[:, qrc, idx * 512:(idx + 1) * 512],
                                                start=(qrc == 0), stop=(qrc == QC - 1))
                        elif kind == "kn":
                            for kc in range(KC):
                                last = t.matmul(pq[i % 3][:], wdk[:, b, kc, :],
                                                ckv_bf[:, kc, idx * 512:(idx + 1) * 512],
                                                start=(kc == 0), stop=(kc == KC - 1))
                        else:
                            for kc in range(KC):
                                last = t.matmul(pq[i % 3][:, 0:128], ckv_bf[:, kc, idx * 128:(idx + 1) * 128],
                                                wdv[:, b, kc, :],
                                                start=(kc == 0), stop=(kc == KC - 1))
                        last.then_inc(s_mm, 1)
                    t.wait_ge(s_ac, aP + NQK)
                    t.wait_ge(s_ve, ASM)
                    last_proj = {bi: max(i for i in range(nproj) if i % 3 == bi) for bi in range(3)}
                    last_reader = {0: None, 1: None, 2: None}
                    for gi, (qt, ksp) in enumerate(p1_groups):
                        bi = gi % 3
                        if last_reader[bi] is None:
                            t.wait_ge(s_ac, aP + last_proj[bi] + 1)
                        else:
                            t.wait_ge(s_ve, ASM + last_reader[bi] + 1)
                        n = min(512, 128 * (qt + 1) - ksp * 512)
                        dst = pq[bi][:, 0:n]
                        t.matmul(dst, qn_h[:, b, qt * 128:(qt + 1) * 128],
                                 kn_h[:, b, ksp * 512:ksp * 512 + n], start=True, stop=False)
                        t.matmul(dst, q_ext[0:64, b, qt * 128:(qt + 1) * 128],
                                 kx_rope[0:64, ksp * 512:ksp * 512 + n], start=False, stop=True).then_inc(s_mm, 1)
                        last_reader[bi] = gi
                    t.wait_ge(s_ve, MNB)
                    for sp in range(SP):
                        if sp >= 1:
                            t.wait_ge(s_ve, MNB + sp)
                        for j2 in range(4):
                            qt = 4 * sp + j2
                            last = t.matmul(pq[0][0:1, j2 * 128:(j2 + 1) * 128], mneg16b[:, qt:qt + 1],
                                            idbf[:], start=True, stop=True)
                        last.then_inc(s_mm, 1)
            c.add("mm", (2 * NKT if has_p2 else 0) + ((nproj + NG1 + SP) if has_p1 else 0))

            @blk.scalar
            def _(sc, jobs=jobs, b=b, te_b=te_b, evN=evN, mP=mP,
                  has_p1=has_p1, has_p2=has_p2):
                if has_p2:
                    for sp in range(SP):
                        nkt = spans_nkt[sp]
                        st_i, pv_i = emission_idx(nkt)
                        for kt in range(nkt):
                            sc.wait_ge(s_mm, te_b[sp] + st_i[kt])
                            if kt >= 4:
                                sc.wait_ge(s_mm, te_b[sp] + pv_i[kt - 4])
                            sc.activation(pt_sb[:, kt % 4, :], ppt[kt % 3][:], AF.Exp).then_inc(s_ac, 1)
                        sc.wait_ge(s_mm, te_b[sp] + 2 * nkt)
                        if sp >= 2:
                            sc.wait_ge(s_ve, evN[sp - 2])
                        sc.activation(o_f32[:, sp % 2, :], po[:], AF.Identity).then_inc(s_ac, 1)
                if has_p1:
                    for i, (kind, idx) in enumerate(jobs):
                        sc.wait_ge(s_mm, mP + i + 1)
                        if kind == "qn":
                            dst, srcp = qn_h[:, b, idx * 512:(idx + 1) * 512], pq[i % 3][:]
                        elif kind == "qev":
                            dst, srcp = qev_raw[:, idx * 512:(idx + 1) * 512], pq[i % 3][:]
                        elif kind == "qod":
                            dst, srcp = qod_raw[:, idx * 512:(idx + 1) * 512], pq[i % 3][:]
                        elif kind == "kn":
                            dst, srcp = kn_h[:, b, idx * 512:(idx + 1) * 512], pq[i % 3][:]
                        else:
                            dst, srcp = v_h[:, b, idx * 128:(idx + 1) * 128], pq[i % 3][:, 0:128]
                        sc.activation(dst, srcp, AF.Identity).then_inc(s_ac, 1)
            c.add("ac", ((NKT + SP) if has_p2 else 0) + (nproj if has_p1 else 0))

            @blk.vector
            def _(v, k=k, h=h, j=j, b=b, te_b=te_b, ac_b=ac_b,
                  mB=mB, aP=aP, vP=vP, ASM=ASM, SCL0=SCL0, gW=gW, nw=nw,
                  bldg=bldg, blds=blds, has_p1=has_p1, has_p2=has_p2):
                v = VG(v)
                if has_p2:
                    def emit_norm(sp):
                        v.wait_ge(s_ldg, bldg + 16 * (sp + 1))
                        v.wait_ge(s_ac, ac_b[sp] + spans_nkt[sp] + 1)
                        if sp > 1:
                            v.wait_ge(s_lds, blds + 32 * (sp - 1))
                        v.tensor_tensor(o_bf[:, sp % 2, :], o_f32[:, sp % 2, :],
                                        rl_bc[:, sp % 2, :], ALU.mult).then_inc(s_ve, 1)

                    for sp in range(SP):
                        nkt = spans_nkt[sp]
                        for d in range(4):
                            kt = 4 * sp + d
                            v.wait_ge(s_ac, ac_b[sp] + kt + 1)
                            v.tensor_tensor(pt_sb[:, kt % 4, :], pt_sb[:, kt % 4, :],
                                            dmask_sb[:, d, :], ALU.mult).then_inc(s_ve, 1)
                        v.wait_ge(s_mm, te_b[sp] + 2 * nkt)
                        if sp > 1:
                            v.wait_ge(s_lds, blds + 32 * (sp - 2) + 16)
                        v.reciprocal(rl_row[0:1, sp % 2, :],
                                     pl[32 * (sp % 2):32 * (sp % 2) + 1, :]).then_inc(s_ve, 1)
                        if sp >= 1:
                            emit_norm(sp - 1)
                    emit_norm(SP - 1)
                if has_p1:
                    if k == 0:
                        v.wait_ge(s_ldg, gW + 64)
                        head_weight_scale(v, 0).then_inc(s_ve, 1)
                        v.tensor_tensor(bsc1[0:32, :], kev_raw[:], tabs[0:32, 2, :], ALU.mult)
                        v.tensor_tensor(bsc2[0:32, :], kod_raw[:], tabs[0:32, 3, :], ALU.mult)
                        v.tensor_tensor(kx_rope[0:32, :], bsc1[0:32, :], bsc2[0:32, :], ALU.subtract)
                        v.tensor_tensor(bsc1[0:32, :], kev_raw[:], tabs[0:32, 3, :], ALU.mult)
                        v.tensor_tensor(bsc2[0:32, :], kod_raw[:], tabs[0:32, 2, :], ALU.mult)
                        v.tensor_tensor(bsc1[0:32, :], bsc1[0:32, :], bsc2[0:32, :], ALU.add)
                        v.tensor_copy(out=kx_rope[32:64, :], in_=bsc1[0:32, :]).then_inc(s_ve, 1)
                    if j == 0:
                        v.wait_ge(s_ac, aP + 3 * SP)
                        v.tensor_tensor(bsc1[:], qev_raw[:], tabs[:, 0, :], ALU.mult)
                        v.tensor_tensor(bsc2[:], qod_raw[:], tabs[:, 1, :], ALU.mult)
                        v.tensor_tensor(qev_rot[:], bsc1[:], bsc2[:], ALU.subtract)
                        v.tensor_tensor(bsc1[:], qev_raw[:], tabs[:, 1, :], ALU.mult)
                        v.tensor_tensor(bsc2[:], qod_raw[:], tabs[:, 0, :], ALU.mult)
                        v.tensor_tensor(qod_rot[:], bsc1[:], bsc2[:], ALU.add)
                    v.tensor_copy(out=q_ext[0:32, b, :], in_=qev_rot[32 * j:32 * j + 32, :])
                    v.tensor_copy(out=q_ext[32:64, b, :], in_=qod_rot[32 * j:32 * j + 32, :]).then_inc(s_ve, 1)
                    for gi, (qt, ksp) in enumerate(p1_groups):
                        v.wait_ge(s_mm, mB + gi + 1)
                        n = min(512, 128 * (qt + 1) - ksp * 512)
                        bi = gi % 3
                        if ksp == 0:
                            last = v.reduce_max(small[:, 5:6], pq[bi][:, 0:n], axis=AX.X)
                        else:
                            v.reduce_max(small[:, 6:7], pq[bi][:, 0:n], axis=AX.X)
                            last = v.tensor_max(small[:, 5:6], small[:, 5:6], small[:, 6:7])
                        if gi == NG1 - 1 or p1_groups[gi + 1][0] != qt:
                            last = v.tensor_scalar(mneg16[:, qt:qt + 1], small[:, 5:6], -1.0,
                                                   None, ALU.mult)
                        last.then_inc(s_ve, 1)
                    v.tensor_copy(out=mneg16b[:, 0:TT], in_=mneg16[:, 0:TT]).then_inc(s_ve, 1)
                    for sp in range(SP):
                        v.wait_ge(s_mm, mB + NG1 + sp + 1)
                        v.tensor_copy(out=q_ext[64:65, b, sp * 512:(sp + 1) * 512],
                                      in_=pq[0][0:1, :]).then_inc(s_ve, 1)
                    if h + 1 < NH:
                        v.wait_ge(s_ldg, gW + nw)
                        head_weight_scale(v, h + 1)
            c.add("ve", (6 * SP if has_p2 else 0) +
                  ((SCL0 + 1 + NG1 + 1 + SP) if has_p1 else 0))

    if CUT == "b2":
        return _finish(nc, c, ctx, [ctx_att], s_lds, s_ldg, s_out, _ncd)

    ctx_att.close()

    ctx_ffn = ExitStack()
    fsb = lambda n, shp, dt=F32: ctx_ffn.enter_context(nc.sbuf_tensor(n, shp, dt))
    wr_bc = fsb("wr_bc", [128, D], F32)
    br_col = fsb("br_col", [128, 1], F32)

    with nc.Block() as blk, ExitStack() as lc:
        lsb = lambda n, shp, dt=F32: lc.enter_context(nc.sbuf_tensor(n, shp, dt))
        w_o_bf = lsb("w_o_bf", [128, NH, D], BF16)
        o_tile = lsb("o_tile", [128, 2, NH * 128], BF16)
        x0t = lsb("x0t3", [128, 2, D], F32)
        a_sb = lsb("a_sb", [128, 2, D], F32)
        acc4 = lsb("acc4", [128, 2, 8], F32)
        x2t = lsb("x2t", [128, 2, D], F32)
        y32 = lsb("y32", [128, D], F32)
        statb = lsb("statb", [128, 2, 4], F32)
        bn2 = lsb("bn23", [128, 2], F32)
        rc1 = lsb("rc1", [128, 1], F32)
        mr1 = lsb("mr1", [128, 1], F32)
        p_a = [lc.enter_context(nc.psum_tensor("p_a%d" % k, [128, 512], F32))
               for k in range(8)]
        bmm, bve, bac, blds, bldg = (c.get("mm"), c.get("ve"), c.get("ac"),
                                     c.get("lds"), c.get("ldg"))
        NA = DSP + 5
        NV = 5

        def posH(tt):
            return 0 if tt == 0 else (DSP + 1) + NA * (tt - 1)

        def posT(tt):
            return 2 * (DSP + 1) + NA * tt if tt <= TT - 2 else NA * TT - 4

        @blk.gpsimd
        def _(g):
            g.dma_start(out=wr_bc[:],
                        in_=W_r.rearrange("d u -> u d").to_broadcast([128, D])).then_inc(s_ldg, 16)
            g.dma_start(out=br_col[:],
                        in_=b_r.unsqueeze(0).to_broadcast([128, 1])).then_inc(s_ldg, 16)
            g.dma_start(out=w_o_bf[:, 0:NH // 2, :],
                        in_=wob_s[0:(NH // 2) * 128, :].rearrange("(c p) d -> p c d", p=128)).then_inc(s_ldg, 16)
            for tt in range(TT):
                g.wait_ge(s_ve, bve + NV * tt + 2)
                g.dma_start(out=x2_s[tt * 128:(tt + 1) * 128, :],
                            in_=x2t[:, tt % 2, :]).then_inc(s_ldg, 16)
                g.wait_ge(s_ve, bve + NV * tt + 4)
                g.dma_start(out=y_s[tt * 128:(tt + 1) * 128, :], in_=y32[:]).then_inc(s_ldg, 16)
        c.add("ldg", 48 + TT * 32)

        @blk.sync
        def _(sy):
            sy.wait_ge(s_ldg, bldg)
            sy.dma_start(out=w_o_bf[:, NH // 2:NH, :],
                         in_=wob_s[(NH // 2) * 128:NH * 128, :].rearrange("(c p) d -> p c d", p=128)).then_inc(s_lds, 16)
            sy.wait_ge(s_lds, blds)
            for tt in range(TT):
                if tt >= 2:
                    sy.wait_ge(s_mm, bmm + (tt - 1) * DSP)
                sy.dma_start(out=o_tile[:, tt % 2, :].rearrange("v (h t) -> v h t", t=128),
                             in_=o_s[tt].transpose([1, 0, 2])).then_inc(s_lds, 16)
                if tt >= 2:
                    sy.wait_ge(s_ve, bve + NV * (tt - 2) + 2)
                sy.dma_start(out=x0t[:, tt % 2, :], in_=x_in[tt * 128:(tt + 1) * 128, :]).then_inc(s_lds, 16)
                if DBG_SCRATCH and tt == 1:
                    sy.wait_ge(s_ve, bve + NV * tt + 2)
                    sy.dma_start(out=adbg[:], in_=a_sb[:, tt % 2, :]).then_inc(s_out, 16)
        c.add("lds", 16 + TT * 2 * 16)
        if DBG_SCRATCH:
            c.add("out", 16)

        @blk.tensor
        def _(t):
            t.wait_ge(s_ldg, bldg + 48)
            t.wait_ge(s_lds, blds + 16)
            for tt in range(TT):
                t.wait_ge(s_lds, blds + 16 + tt * 32 + 16)
                for dsp in range(DSP):
                    i = tt * DSP + dsp
                    if i >= 8:
                        j = i - 8
                        t.wait_ge(s_ac, bac + posH(j // DSP) + (j % DSP) + 1)
                    for hh in range(NH):
                        last = t.matmul(p_a[i % 8][:], o_tile[:, tt % 2, hh * 128:(hh + 1) * 128],
                                        w_o_bf[:, hh, dsp * 512:(dsp + 1) * 512],
                                        start=(hh == 0), stop=(hh == NH - 1))
                    last.then_inc(s_mm, 1)
        c.add("mm", TT * DSP)

        @blk.scalar
        def _(sc):

            def emit_H(tt):
                if tt >= 2:
                    sc.wait_ge(s_ve, bve + NV * (tt - 2) + 5)
                for dsp in range(DSP):
                    sc.wait_ge(s_mm, bmm + tt * DSP + dsp + 1)
                    sc.activation(a_sb[:, tt % 2, dsp * 512:(dsp + 1) * 512],
                                  p_a[(tt * DSP + dsp) % 8][:], AF.Identity,
                                  accum_out=acc4[:, tt % 2, dsp:dsp + 1]).then_inc(s_ac, 1)
                if tt >= 2:
                    sc.wait_ge(s_ve, bve + NV * (tt - 2) + 4)
                    sc.wait_ge(s_ldg, bldg + 48 + 32 * (tt - 2) + 16)
                sc.activation(x2t[:, tt % 2, :], a_sb[:, tt % 2, :], AF.Square,
                              accum_out=acc4[:, tt % 2, DSP:DSP + 1]).then_inc(s_ac, 1)

            def emit_T(tt):
                sc.wait_ge(s_ve, bve + NV * tt + 1)
                sc.activation(statb[:, tt % 2, 1:2], statb[:, tt % 2, 0:1], AF.Sqrt).then_inc(s_ac, 1)
                sc.wait_ge(s_ve, bve + NV * tt + 2)
                sc.activation(a_sb[:, tt % 2, :], x2t[:, tt % 2, :], AF.Square,
                              accum_out=acc4[:, tt % 2, DSP + 1:DSP + 2]).then_inc(s_ac, 1)
                sc.activation(a_sb[:, tt % 2, :], x2t[:, tt % 2, :], AF.Identity,
                              accum_out=acc4[:, tt % 2, DSP + 2:DSP + 3]).then_inc(s_ac, 1)
                sc.wait_ge(s_ve, bve + NV * tt + 3)
                sc.activation(statb[:, tt % 2, 3:4], statb[:, tt % 2, 2:3], AF.Sqrt).then_inc(s_ac, 1)

            for tt in range(TT):
                emit_H(tt)
                if tt >= 1:
                    emit_T(tt - 1)
            emit_T(TT - 1)
        c.add("ac", TT * NA)

        @blk.vector
        def _(v):
            v = VG(v)
            for tt in range(TT):
                v.wait_ge(s_ac, bac + posH(tt) + DSP + 1)
                a4 = acc4[:, tt % 2, :]
                if DSP == 4:
                    v.tensor_add(small[:, 4:5], a4[:, 0:1], a4[:, 1:2])
                    v.tensor_add(small[:, 5:6], a4[:, 2:3], a4[:, 3:4])
                    v.tensor_add(small[:, 4:5], small[:, 4:5], small[:, 5:6])
                elif DSP == 2:
                    v.tensor_add(small[:, 4:5], a4[:, 0:1], a4[:, 1:2])
                else:
                    v.tensor_copy(out=small[:, 4:5], in_=a4[:, 0:1])
                v.tensor_scalar(bn2[:, 0:1], small[:, 4:5], 1.0 / D, None, ALU.mult)
                v.tensor_scalar(bn2[:, 1:2], a4[:, DSP:DSP + 1], 1.0 / D, None, ALU.mult)
                v.tensor_mul(small[:, 3:4], bn2[:, 0:1], bn2[:, 0:1])
                v.tensor_tensor(bn2[:, 1:2], bn2[:, 1:2], small[:, 3:4], ALU.subtract)
                v.tensor_scalar(statb[:, tt % 2, 0:1], bn2[:, 1:2], float(LN_EPS), None, ALU.add).then_inc(s_ve, 1)
                v.wait_ge(s_ac, bac + posT(tt) + 1)
                v_rsqrt(v, statb[:, tt % 2, 0:1], statb[:, tt % 2, 1:2], rc1[:], small[:, 2:3])
                v.tensor_mul(mr1[:], bn2[:, 0:1], rc1[:])
                v.tensor_scalar(mr1[:], mr1[:], -1.0, None, ALU.mult)
                v.wait_ge(s_lds, blds + 16 + 32 * tt + 32)
                v.scalar_tensor_tensor(x2t[:, tt % 2, :], a_sb[:, tt % 2, :], rc1[:],
                                       x0t[:, tt % 2, :], ALU.mult, ALU.add)
                v.tensor_scalar(x2t[:, tt % 2, :], x2t[:, tt % 2, :], mr1[:], None, ALU.add).then_inc(s_ve, 1)
                v.wait_ge(s_ac, bac + posT(tt) + 3)
                v.tensor_scalar(bn2[:, 0:1], a4[:, DSP + 2:DSP + 3], 1.0 / D, None, ALU.mult)
                v.tensor_scalar(bn2[:, 1:2], a4[:, DSP + 1:DSP + 2], 1.0 / D, None, ALU.mult)
                v.tensor_mul(small[:, 3:4], bn2[:, 0:1], bn2[:, 0:1])
                v.tensor_tensor(bn2[:, 1:2], bn2[:, 1:2], small[:, 3:4], ALU.subtract)
                v.tensor_scalar(statb[:, tt % 2, 2:3], bn2[:, 1:2], float(LN_EPS), None, ALU.add).then_inc(s_ve, 1)
                v.wait_ge(s_ac, bac + posT(tt) + 4)
                v_rsqrt(v, statb[:, tt % 2, 2:3], statb[:, tt % 2, 3:4], rc1[:], small[:, 2:3])
                v.tensor_copy(out=r2col[:, tt:tt + 1], in_=rc1[:])
                v.tensor_mul(mr1[:], bn2[:, 0:1], rc1[:])
                v.tensor_scalar(mr1[:], mr1[:], -1.0, None, ALU.mult)
                v.tensor_copy(out=m2r2[:, tt:tt + 1], in_=mr1[:])
                if tt >= 1:
                    v.wait_ge(s_ldg, bldg + 48 + 32 * (tt - 1) + 32)
                v.tensor_scalar(y32[:], x2t[:, tt % 2, :], rc1[:], mr1[:],
                                ALU.mult, ALU.add).then_inc(s_ve, 1)
                if tt == 0:
                    v.wait_ge(s_ldg, bldg + 32)
                v.scalar_tensor_tensor(a_sb[:, tt % 2, :], y32[:], 1.0, wr_bc[:],
                                       ALU.mult, ALU.mult, accum_out=small[:, 6:7])
                v.tensor_tensor(w16[:, tt:tt + 1], small[:, 6:7], br_col[:], ALU.add).then_inc(s_ve, 1)
        c.add("ve", TT * NV)

    if CUT == "b3":
        return _finish(nc, c, ctx, [ctx_ffn], s_lds, s_ldg, s_out, _ncd)

    W_all = fsb("W_all", [128, S], F32)
    rowC = fsb("rowC", [1, S], F32)
    wg_bc = fsb("wg_bc", [128, KSEL], F32)
    rt_bc = fsb("rt_bc", [128, S], F32)
    rt16 = fsb("rt16", [128, TT], F32)
    wselc = fsb("wselc", [128, SC], F32)
    P_sb = fsb("P_sb", [128, TT, KSEL], BF16)
    PT_sb = fsb("PT_sb", [128, SC, S], BF16)
    b1cols = fsb("b1cols", [128, FC], F32)
    b2cols = fsb("b2cols", [128, DC], F32)

    with nc.Block() as blk, ExitStack() as lc:
        scr2 = lc.enter_context(nc.sbuf_tensor("scr2", [128, S], F32))
        cnt16 = lc.enter_context(nc.sbuf_tensor("cnt16", [128, TT], F32))
        blds, bldg, bve = c.get("lds"), c.get("ldg"), c.get("ve")

        @blk.sync
        def _(sy):
            sy.dma_start(out=wrow_s.squeeze(0).rearrange("(cc p) -> p cc", p=128),
                         in_=w16[:]).then_inc(s_lds, 16)
        c.add("lds", 16)

        @blk.gpsimd
        def _(g):
            g.wait_ge(s_lds, blds + 16)
            g.dma_start(out=W_all[:], in_=wrow_s.to_broadcast([128, S])).then_inc(s_ldg, 16)
        c.add("ldg", 16)

        @blk.vector
        def _(v):
            v = VG(v)
            v.wait_ge(s_ldg, bldg + 16)
            for tc in range(TT):
                v._v.tensor_scalar(scr2[:], W_all[:], w16[:, tc:tc + 1], None,
                                   ALU.is_ge, ALU.add, accum_out=cnt16[:, tc:tc + 1])
            v.drain()
            v.tensor_single_scalar(selc[:], cnt16[:], KSEL - 0.5, ALU.is_le)
            v.tensor_scalar(nselc[:], selc[:], -1.0, 1.0, ALU.mult, ALU.add)
        c.add("ve", 0)

    if CUT == "b4a":
        return _finish(nc, c, ctx, [ctx_ffn], s_lds, s_ldg, s_out, _ncd)

    with nc.Block() as blk:
        blds, bve, bldg = c.get("lds"), c.get("ve"), c.get("ldg")

        @blk.sync
        def _(sy):
            sy.dma_start(out=selrow_s.squeeze(0).rearrange("(cc p) -> p cc", p=128),
                         in_=selc[:]).then_inc(s_lds, 16)
            sy.wait_ge(s_lds, blds + 16)
            sy.dma_start(out=rowB[0:1, 0:S], in_=selrow_s[:]).then_inc(s_lds, 16)
            sy.wait_ge(s_ve, bve + 1)
            sy.dma_start(out=rt_s[:], in_=rowA[0:1, 0:S]).then_inc(s_lds, 16)
            sy.wait_ge(s_lds, blds + 48)
            sy.dma_start(out=rt16[:], in_=rt_s.squeeze(0).rearrange("(cc p) -> p cc", p=128)).then_inc(s_lds, 16)
        c.add("lds", 64)

        @blk.vector
        def _(v):
            v = VG(v)
            v.memset(rowC[:], 1.0)
            v.wait_ge(s_lds, blds + 32)
            v.tensor_tensor_scan(rowA[0:1, :], rowC[0:1, :], rowB[0:1, :], 0.0,
                                 ALU.mult, ALU.add)
            v.tensor_tensor(rowA[0:1, :], rowA[0:1, :], rowB[0:1, :], ALU.mult)
            v.tensor_scalar(rowA[0:1, :], rowA[0:1, :], -1.0, None, ALU.add).then_inc(s_ve, 1)
            v.wait_ge(s_ldg, bldg + 16)
            for sc_i in range(SC):
                v._v.tensor_scalar(PT_sb[:, sc_i, :], rt_bc[:], iotacols_sb[:, sc_i:sc_i + 1],
                                   None, ALU.is_equal)
            v.wait_ge(s_lds, blds + 64)
            for tc in range(TT):
                last = v._v.tensor_scalar(P_sb[:, tc, :], iotak_sb[:], rt16[:, tc:tc + 1],
                                          None, ALU.is_equal)
            last.then_inc(s_ve, 1)
        c.add("ve", 2)

        @blk.gpsimd
        def _(g):
            g.wait_ge(s_lds, blds + 48)
            g.dma_start(out=rt_bc[:], in_=rt_s.to_broadcast([128, S])).then_inc(s_ldg, 16)
        c.add("ldg", 16)

    if CUT == "b4b":
        return _finish(nc, c, ctx, [ctx_ffn], s_lds, s_ldg, s_out, _ncd)

    with nc.Block() as blk, ExitStack() as lc:
        scr2 = lc.enter_context(nc.sbuf_tensor("scr2b", [128, S], F32))
        blds, bve, bldg = c.get("lds"), c.get("ve"), c.get("ldg")

        @blk.vector
        def _(v):
            v = VG(v)
            for sc_i in range(SC):
                v._v.scalar_tensor_tensor(scr2[:], rt_bc[:], iotacols_sb[:, sc_i:sc_i + 1],
                                          W_all[:], ALU.is_equal, ALU.mult,
                                          accum_out=wselc[:, sc_i:sc_i + 1])
            v.drain()
            v._v.tensor_scalar(small[:, 7:8], wselc[:, 0:1], 1.0, None,
                               ALU.mult).then_inc(s_ve, 1)
        c.add("ve", 1)

        @blk.sync
        def _(sy):
            sy.wait_ge(s_ve, bve + 1)
            sy.dma_start(out=wselrow_s.squeeze(0).rearrange("(cc p) -> p cc", p=128),
                         in_=wselc[:]).then_inc(s_lds, 16)
            sy.dma_start(out=b1cols[:], in_=b1.rearrange("(c p) -> p c", p=128)).then_inc(s_lds, 16)
            sy.dma_start(out=b2cols[:], in_=b2.rearrange("(c p) -> p c", p=128)).then_inc(s_lds, 16)
        c.add("lds", 48)

        @blk.gpsimd
        def _(g):
            g.wait_ge(s_lds, blds + 16)
            g.dma_start(out=wg_bc[:], in_=wselrow_s.to_broadcast([128, KSEL])).then_inc(s_ldg, 16)
        c.add("ldg", 16)

    if CUT == "b4c":
        return _finish(nc, c, ctx, [ctx_ffn], s_lds, s_ldg, s_out, _ncd)

    ctx_ffn2 = ExitStack()
    xg_sb = ctx_ffn2.enter_context(nc.sbuf_tensor("xg_sb", [128, DC, KSEL], BF16))
    with nc.Block() as blk, ExitStack() as lc:
        y_t = lc.enter_context(nc.sbuf_tensor("y_t", [128, TT, D], BF16))
        p_x = [lc.enter_context(nc.psum_tensor("p_x%d" % k, [128, KSEL], F32))
               for k in range(8)]
        blds, bldg, bmm, bac = c.get("lds"), c.get("ldg"), c.get("mm"), c.get("ac")
        DH2 = DC // 2

        @blk.sync
        def _(sy):
            sy.wait_ge(s_ldg, bldg)
            for tc in range(1, TT, 2):
                sy.dma_start(out=y_t[:, tc, :],
                             in_=y_s[tc * 128:(tc + 1) * 128, :]).then_inc(s_lds, 16)
        c.add("lds", (TT // 2) * 16)

        @blk.gpsimd
        def _(g):
            for tc in range(0, TT, 2):
                g.dma_start(out=y_t[:, tc, :],
                            in_=y_s[tc * 128:(tc + 1) * 128, :]).then_inc(s_ldg, 16)
        c.add("ldg", (TT - TT // 2) * 16)

        @blk.tensor
        def _(t):
            for half in range(2):
                for tc in range(TT):
                    if half == 0:
                        if tc % 2 == 0:
                            t.wait_ge(s_ldg, bldg + (tc // 2 + 1) * 16)
                        else:
                            t.wait_ge(s_lds, blds + (tc // 2 + 1) * 16)
                    if tc == 0 and half == 1:
                        t.wait_ge(s_ac, bac + DH2)
                    for dc8 in range(DH2):
                        dc = half * DH2 + dc8
                        last = t.matmul(p_x[dc8][:], y_t[:, tc, dc * 128:(dc + 1) * 128],
                                        P_sb[:, tc, :], start=(tc == 0), stop=(tc == TT - 1))
                    last.then_inc(s_mm, 1)
        c.add("mm", 2 * TT)

        @blk.scalar
        def _(sc):
            for half in range(2):
                sc.wait_ge(s_mm, bmm + (half + 1) * TT)
                for dc8 in range(DH2):
                    dc = half * DH2 + dc8
                    sc.activation(xg_sb[:, dc, :], p_x[dc8][:], AF.Identity).then_inc(s_ac, 1)
        c.add("ac", DC)

    if CUT == "b5a":
        return _finish(nc, c, ctx, [ctx_ffn2, ctx_ffn], s_lds, s_ldg, s_out, _ncd)

    h_sb = ctx_ffn2.enter_context(nc.sbuf_tensor("h_sb", [128, FC, KSEL], BF16))
    with nc.Block() as blk, ExitStack() as lc:
        w1sl = lc.enter_context(nc.sbuf_tensor("w1sl", [128, 2, DC, 2, 128], BF16))
        p_h = [lc.enter_context(nc.psum_tensor("p_h%d" % k, [128, KSEL], F32))
               for k in range(2)]
        bldg, bmm, bac = c.get("ldg"), c.get("mm"), c.get("ac")
        blds = c.get("lds")
        blds_cols = c.get("lds")

        @blk.sync
        def _(sy):
            sy.wait_ge(s_ldg, bldg)
            for fc2 in range(0, FC, 2):
                if fc2 >= 4:
                    sy.wait_ge(s_mm, bmm + fc2 - 2)
                sy.dma_start(out=w1sl[:, (fc2 // 2) % 2, :, :, :],
                             in_=w1b_s[:, fc2 * 128:(fc2 + 2) * 128].rearrange("(c p) (two m) -> p c two m", p=128, two=2)).then_inc(s_lds, 16)
        c.add("lds", (FC // 2) * 16)

        @blk.tensor
        def _(t):
            for fc in range(FC):
                t.wait_ge(s_lds, blds + (fc // 2 + 1) * 16)
                if fc >= 2:
                    t.wait_ge(s_ac, bac + fc - 1)
                for dc in range(DC):
                    last = t.matmul(p_h[fc % 2][:], w1sl[:, (fc // 2) % 2, dc, fc % 2, :], xg_sb[:, dc, :],
                                    start=(dc == 0), stop=(dc == DC - 1))
                last.then_inc(s_mm, 1)
        c.add("mm", FC)

        @blk.scalar
        def _(sc):
            sc.wait_ge(s_lds, blds_cols)
            for fc in range(FC):
                sc.wait_ge(s_mm, bmm + fc + 1)
                sc.activation(h_sb[:, fc, :], p_h[fc % 2][:], AF.Gelu_apprx_tanh,
                              bias=b1cols[:, fc:fc + 1]).then_inc(s_ac, 1)
        c.add("ac", FC)

    if CUT == "b5b":
        return _finish(nc, c, ctx, [ctx_ffn2, ctx_ffn], s_lds, s_ldg, s_out, _ncd)

    with nc.Block() as blk, ExitStack() as lc:
        w2sl = lc.enter_context(nc.sbuf_tensor("w2sl", [128, 4, 2, 1024], BF16))
        vf = lc.enter_context(nc.sbuf_tensor("vf", [128, KSEL], F32))
        vb = lc.enter_context(nc.sbuf_tensor("vb", [128, DC, KSEL], BF16))
        p_v = [lc.enter_context(nc.psum_tensor("p_v%d" % k, [128, KSEL], F32))
               for k in range(8)]
        bldg, bmm, bac, bve, blds = (c.get("ldg"), c.get("mm"), c.get("ac"),
                                     c.get("ve"), c.get("lds"))
        blds_w = blds
        DH2 = DC // 2

        @blk.gpsimd
        def _(g):
            for half in range(2):
                for fc2 in range(0, FC, 2):
                    i = half * (FC // 2) + fc2 // 2
                    if i % 2 == 1:
                        continue
                    if i >= 4:
                        g.wait_ge(s_mm, bmm + 2 * i - 6)
                    g.dma_start(out=w2sl[:, i % 4, :, 0:DH2 * 128],
                                in_=w2b_s[fc2 * 128:(fc2 + 2) * 128,
                                          half * DH2 * 128:(half + 1) * DH2 * 128].rearrange(
                                              "(two p) m -> p two m", two=2)).then_inc(s_ldg, 16)

        @blk.sync
        def _(sy):
            for half in range(2):
                for fc2 in range(0, FC, 2):
                    i = half * (FC // 2) + fc2 // 2
                    if i % 2 == 0:
                        continue
                    if i >= 4:
                        sy.wait_ge(s_mm, bmm + 2 * i - 6)
                    sy.dma_start(out=w2sl[:, i % 4, :, 0:DH2 * 128],
                                 in_=w2b_s[fc2 * 128:(fc2 + 2) * 128,
                                           half * DH2 * 128:(half + 1) * DH2 * 128].rearrange(
                                               "(two p) m -> p two m", two=2)).then_inc(s_lds, 16)
        _ng = sum(1 for _h in range(2) for _f in range(0, FC, 2)
                  if ((_h * (FC // 2) + _f // 2) % 2 == 0))
        _ns = 2 * (FC // 2) - _ng
        c.add("ldg", _ng * 16)
        c.add("lds", _ns * 16)

        @blk.tensor
        def _(t):
            for half in range(2):
                for fc in range(FC):
                    i = half * FC + fc
                    ld = half * (FC // 2) + fc // 2
                    if ld % 2 == 0:
                        t.wait_ge(s_ldg, bldg + (ld // 2 + 1) * 16)
                    else:
                        t.wait_ge(s_lds, blds_w + (ld // 2 + 1) * 16)
                    if fc == 0 and half == 1:
                        t.wait_ge(s_ve, bve + DH2)
                    for dc8 in range(DH2):
                        last = t.matmul(p_v[dc8][:], w2sl[:, ld % 4, fc % 2, dc8 * 128:(dc8 + 1) * 128],
                                        h_sb[:, fc, :], start=(fc == 0), stop=(fc == FC - 1))
                    last.then_inc(s_mm, 1)
        c.add("mm", 2 * FC)

        @blk.scalar
        def _(sc):
            for half in range(2):
                sc.wait_ge(s_mm, bmm + (half + 1) * FC)
                for dc8 in range(DH2):
                    dc = half * DH2 + dc8
                    sc.wait_ge(s_ve, bve + half * DH2 + dc8)
                    sc.activation(vf[:], p_v[dc8][:], AF.Identity,
                                  bias=b2cols[:, dc:dc + 1]).then_inc(s_ac, 1)
        c.add("ac", DC)

        @blk.vector
        def _(v):
            v = VG(v)
            v.wait_ge(s_ldg, bldg)
            for half in range(2):
                for dc8 in range(DH2):
                    i = half * DH2 + dc8
                    v.wait_ge(s_ac, bac + i + 1)
                    v.tensor_tensor(vb[:, i, :], vf[:], wg_bc[:], ALU.mult).then_inc(s_ve, 1)
        c.add("ve", DC)

        @blk.sync
        def _(sy):
            for half in range(2):
                for dc8 in range(DH2):
                    i = half * DH2 + dc8
                    dc = half * DH2 + dc8
                    sy.wait_ge(s_ve, bve + i + 1)
                    sy.dma_start(out=vals_s[dc * 128:(dc + 1) * 128, :], in_=vb[:, i, :]).then_inc(s_lds, 16)
        c.add("lds", DC * 16)

    if CUT == "b5c":
        return _finish(nc, c, ctx, [ctx_ffn2, ctx_ffn], s_lds, s_ldg, s_out, _ncd)

    ctx_ffn2.close()

    vals_sm = fsb("vals_sm", [128, SC, D], BF16)
    with nc.Block() as blk:
        blds = c.get("lds")

        @blk.sync
        def _(sy):
            sy.wait_ge(s_lds, blds)
            for sc_i in range(SC):
                sy.dma_start_transpose(out=vals_sm[:, sc_i, :],
                                       in_=vals_s[:, sc_i * 128:(sc_i + 1) * 128]).then_inc(s_lds, 16)
            sy.wait_ge(s_lds, blds + SC * 16)
        c.add("lds", SC * 16)

    if CUT == "b5d":
        return _finish(nc, c, ctx, [ctx_ffn], s_lds, s_ldg, s_out, _ncd)

    with nc.Block() as blk, ExitStack() as lc:
        lsb = lambda n, shp, dt=F32: lc.enter_context(nc.sbuf_tensor(n, shp, dt))
        x2t = lsb("x2t_e", [128, 3, D], F32)
        fproc = lsb("fproc", [128, 2, D], F32)
        y32 = lsb("y32_e", [128, 2, D], F32)
        ff = lsb("ff", [128, 3, D], F32)
        outt = lsb("outt", [128, D], F32)
        acc4e2 = lsb("acc4e2", [128, 2, 2], F32)
        statbe = lsb("statbe", [128, 2, 2], F32)
        bn2 = lsb("bn2_e", [128, 3, 2], F32)
        rc1 = lsb("rc1_e", [128, 1], F32)
        mr1 = lsb("mr1_e", [128, 1], F32)
        p_f = [lc.enter_context(nc.psum_tensor("p_f%d" % k, [128, 512], F32))
               for k in range(8)]
        bmm, bve, bac, blds = c.get("mm"), c.get("ve"), c.get("ac"), c.get("lds")
        bout = c.get("out")
        NAe = DSP + 3
        NVe = 3

        def posHe(tc):
            return 0 if tc == 0 else (DSP + 1) + NAe * (tc - 1)

        def posTe(tc):
            return 2 * (DSP + 1) + NAe * tc if tc <= TT - 2 else NAe * TT - 2

        dve_ops = [("A", 0), ("A", 1)]
        for _tc in range(TT):
            dve_ops.append(("S", _tc))
            if _tc + 2 < TT:
                dve_ops.append(("A", _tc + 2))
            dve_ops.append(("C", _tc))
        evA, evS, evC = {}, {}, {}
        for _i, (_kind, _tc) in enumerate(dve_ops):
            {"A": evA, "S": evS, "C": evC}[_kind][_tc] = bve + _i + 1

        bldg5 = c.get("ldg")

        @blk.gpsimd
        def _(g):
            for tc in range(TT):
                if tc >= 3:
                    g.wait_ge(s_ve, evC[tc - 3])
                g.dma_start(out=x2t[:, tc % 3, :],
                            in_=x2_s[tc * 128:(tc + 1) * 128, :]).then_inc(s_ldg, 16)
        c.add("ldg", TT * 16)

        @blk.sync
        def _(sy):
            for tc in range(TT):
                if DEBUG_FPROC:
                    sy.wait_ge(s_ve, evA[tc])
                    sy.dma_start(out=fdbg[tc], in_=fproc[:, tc % 2, :]).then_inc(s_out, 16)
                sy.wait_ge(s_ve, evC[tc])
                sy.dma_start(out=out[tc * 128:(tc + 1) * 128, :], in_=outt[:]).then_inc(s_out, 16)
            sy.wait_ge(s_out, bout + TT * 16 * (2 if DEBUG_FPROC else 1))
        c.add("out", TT * 16 * (2 if DEBUG_FPROC else 1))

        @blk.tensor
        def _(t):
            for tc in range(TT):
                for dsp in range(DSP):
                    i = tc * DSP + dsp
                    if i >= 8:
                        j = i - 8
                        t.wait_ge(s_ac, bac + posHe(j // DSP) + (j % DSP) + 1)
                    for sc_i in range(SC):
                        last = t.matmul(p_f[i % 8][:], PT_sb[:, sc_i, tc * 128:(tc + 1) * 128],
                                        vals_sm[:, sc_i, dsp * 512:(dsp + 1) * 512],
                                        start=(sc_i == 0), stop=(sc_i == SC - 1))
                    last.then_inc(s_mm, 1)
        c.add("mm", TT * DSP)

        @blk.scalar
        def _(sc):

            def emit_H(tc):
                if tc >= 2:
                    sc.wait_ge(s_ve, evA[tc - 2])
                for dsp in range(DSP):
                    sc.wait_ge(s_mm, bmm + tc * DSP + dsp + 1)
                    sc.activation(fproc[:, tc % 2, dsp * 512:(dsp + 1) * 512],
                                  p_f[(tc * DSP + dsp) % 8][:], AF.Identity).then_inc(s_ac, 1)
                sc.wait_ge(s_ldg, bldg5 + (tc + 1) * 16)
                sc.activation(y32[:, tc % 2, :], x2t[:, tc % 3, :], AF.Identity,
                              bias=m2r2[:, tc:tc + 1], scale=r2col[:, tc:tc + 1]).then_inc(s_ac, 1)

            def emit_T(tc):
                sc.wait_ge(s_ve, evA[tc])
                sc.activation(fproc[:, tc % 2, :], ff[:, tc % 3, :], AF.Square,
                              accum_out=acc4e2[:, tc % 2, 0:1]).then_inc(s_ac, 1)
                sc.wait_ge(s_ve, evS[tc])
                sc.activation(statbe[:, tc % 2, 1:2], statbe[:, tc % 2, 0:1], AF.Sqrt).then_inc(s_ac, 1)

            for tc in range(TT):
                emit_H(tc)
                if tc >= 1:
                    emit_T(tc - 1)
            emit_T(TT - 1)
        c.add("ac", TT * NAe)

        @blk.vector
        def _(v):
            v = VG(v)
            def emit_A(tc):
                v.wait_ge(s_ac, bac + posHe(tc) + DSP + 1)
                v.scalar_tensor_tensor(ff[:, tc % 3, :], y32[:, tc % 2, :], nselc[:, tc:tc + 1],
                                       fproc[:, tc % 2, :], ALU.mult, ALU.add,
                                       accum_out=acc4e2[:, tc % 2, 1:2]).then_inc(s_ve, 1)

            def emit_S(tc):
                v.wait_ge(s_ac, bac + posTe(tc) + 1)
                v.tensor_scalar(bn2[:, tc % 3, 0:1], acc4e2[:, tc % 2, 1:2], 1.0 / D, None, ALU.mult)
                v.tensor_scalar(bn2[:, tc % 3, 1:2], acc4e2[:, tc % 2, 0:1], 1.0 / D, None, ALU.mult)
                v.tensor_mul(small[:, 3:4], bn2[:, tc % 3, 0:1], bn2[:, tc % 3, 0:1])
                v.tensor_tensor(bn2[:, tc % 3, 1:2], bn2[:, tc % 3, 1:2], small[:, 3:4], ALU.subtract)
                v.tensor_scalar(statbe[:, tc % 2, 0:1], bn2[:, tc % 3, 1:2], float(LN_EPS), None, ALU.add).then_inc(s_ve, 1)

            def emit_C(tc):
                v.wait_ge(s_ac, bac + posTe(tc) + 2)
                v_rsqrt(v, statbe[:, tc % 2, 0:1], statbe[:, tc % 2, 1:2], rc1[:], small[:, 2:3])
                v.tensor_mul(mr1[:], bn2[:, tc % 3, 0:1], rc1[:])
                v.tensor_scalar(mr1[:], mr1[:], -1.0, None, ALU.mult)
                if tc >= 1:
                    v.wait_ge(s_out, bout + 16 * tc * (2 if DEBUG_FPROC else 1))
                v.scalar_tensor_tensor(outt[:], ff[:, tc % 3, :], rc1[:], x2t[:, tc % 3, :],
                                       ALU.mult, ALU.add)
                v.tensor_scalar(outt[:], outt[:], mr1[:], None, ALU.add).then_inc(s_ve, 1)

            for kind, tc in dve_ops:
                {"A": emit_A, "S": emit_S, "C": emit_C}[kind](tc)
        c.add("ve", TT * NVe)

    ctx_ffn.close()

    with nc.Block() as blk:
        @blk.sync
        def _(sy):
            sy.wait_ge(s_lds, c.get("lds"))
            sy.wait_ge(s_out, c.get("out"))

        @blk.gpsimd
        def _(g):
            g.wait_ge(s_ldg, c.get("ldg"))

    ctx.close()
    _ncd.__exit__(None, None, None)
    _audit(nc, c, {"s_lds": "lds", "s_ldg": "ldg", "s_mm": "mm", "s_ve": "ve",
                   "s_ac": "ac", "s_gp": "gp", "s_out": "out"})
    return nc


def shard_inputs(C, inputs, consts):
    x = np.ascontiguousarray(inputs["L0_attn_x"], dtype=np.float32)
    B = x.shape[0]
    base = {
        "W_cq": inputs["L0_attn_W_cq"], "q_scale": inputs["L0_attn_q_scale"],
        "W_dq_nope": inputs["L0_attn_W_dq_nope"], "W_dq_rope": inputs["L0_attn_W_dq_rope"],
        "W_ckv": inputs["L0_attn_W_ckv"], "kv_scale": inputs["L0_attn_kv_scale"],
        "W_dk_nope": inputs["L0_attn_W_dk_nope"], "W_dv": inputs["L0_attn_W_dv"],
        "W_krope": inputs["L0_attn_W_krope"], "W_o": inputs["L0_attn_W_o"],
        "W_r": inputs["L0_ffn_W_r"], "b_r": inputs["L0_ffn_b_r"],
        "W1": inputs["L0_ffn_W1"], "b1": inputs["L0_ffn_b1"],
        "W2": inputs["L0_ffn_W2"], "b2": inputs["L0_ffn_b2"],
    }
    base = {k: np.ascontiguousarray(v, dtype=np.float32) for k, v in base.items()}
    base.update(consts)
    return [dict(base, x=np.ascontiguousarray(x[i])) for i in range(B)]


def kernel(**inputs):
    C = FULL_CFG
    consts = host_consts(C)
    nc = build_nc(C)
    in_maps = shard_inputs(C, inputs, consts)
    res = run_bass_kernel_spmd(nc, in_maps, core_ids=list(range(len(in_maps))))
    return np.stack([r["out"] for r in res.results], axis=0)
